# version 05
# baseline (speedup 1.0000x reference)
import os
import numpy as np
import ml_dtypes
import concourse.bass as bass
import concourse.mybir as mybir
from concourse.bass_utils import run_bass_kernel_spmd

F32 = mybir.dt.float32
BF16 = mybir.dt.bfloat16
AF = mybir.ActivationFunctionType
ALU = mybir.AluOpType
AXX = mybir.AxisListType.X

T = 2048
D = 1024
NTC = T // 128
NDC = D // 128

BF = np.dtype(ml_dtypes.bfloat16)


class Prog:
    CENG = ("T", "V", "S", "G")
    QENG = ("Y", "G2", "S2", "V2")

    def __init__(self, nc):
        self.nc = nc
        self.ops = []
        self.deps = []
        self.writes_log = {}
        self.reads_log = {}
        self.flag = []
        self.dbg = []

    @staticmethod
    def _key(tensor):
        return tensor.name if hasattr(tensor, "name") else str(id(tensor))

    @staticmethod
    def _region(ap_or_t):
        if isinstance(ap_or_t, tuple):
            return ap_or_t
        ap = ap_or_t
        t = ap.tensor
        key = Prog._key(t)
        esz = mybir.dt.size(ap.dtype)
        off = int(ap.offset)
        aplist = [(int(a), int(b)) for (a, b) in ap.ap]
        cls = type(t).__name__
        if "DRam" in cls:
            span = 1
            for st, ct in aplist:
                span += abs(st) * (ct - 1)
            return (key, 0, 128, off * esz, (off + span) * esz)
        if "PSum" in cls:
            row = 1
            for s in [int(s) for s in t.shape][1:]:
                row *= s
            return (key, 0, 128, 0, row * esz)
        shp = [int(s) for s in t.shape]
        row = 1
        for s in shp[1:]:
            row *= s
        p0 = off // row
        col0 = off % row
        pcnt = aplist[0][1]
        span = 1
        for st, ct in aplist[1:]:
            span += abs(st) * (ct - 1)
        return (key, p0, p0 + pcnt, col0 * esz, (col0 + span) * esz)

    def _record(self, opid, reads, writes):
        dep = set()
        for r in reads:
            k = r[0]
            for w in self.writes_log.get(k, []):
                if w[1] < r[2] and r[1] < w[0 + 1] or True:
                    pass
            for (p0, p1, b0, b1, oid) in self.writes_log.get(k, []):
                if not (b1 <= r[3] or r[4] <= b0 or p1 <= r[1] or r[2] <= p0):
                    dep.add(oid)
            self.reads_log.setdefault(k, []).append((r[1], r[2], r[3], r[4], opid))
        for w in writes:
            k = w[0]
            for (p0, p1, b0, b1, oid) in self.writes_log.get(k, []):
                if not (b1 <= w[3] or w[4] <= b0 or p1 <= w[1] or w[2] <= p0):
                    dep.add(oid)
            for (p0, p1, b0, b1, oid) in self.reads_log.get(k, []):
                if not (b1 <= w[3] or w[4] <= b0 or p1 <= w[1] or w[2] <= p0):
                    dep.add(oid)
            def covered(rec):
                return (w[1] <= rec[0] and rec[1] <= w[2] and
                        w[3] <= rec[2] and rec[3] <= w[4])
            self.writes_log[k] = [rec for rec in self.writes_log.get(k, [])
                                  if not covered(rec)]
            self.reads_log[k] = [rec for rec in self.reads_log.get(k, [])
                                 if not covered(rec)]
            self.writes_log.setdefault(k, []).append((w[1], w[2], w[3], w[4], opid))
        dep.discard(opid)
        return dep

    def op(self, engine, emit, reads=(), writes=(), tag=""):
        opid = len(self.ops)
        rr = [self._region(r) for r in reads]
        ww = [self._region(w) for w in writes]
        self.ops.append((engine, emit, "c"))
        self.deps.append(self._record(opid, rr, ww))
        self.flag.append(False)
        self.dbg.append(tag)
        return opid

    def dma(self, dst, src, queue="Y", reads=(), writes=(), tag=""):
        opid = len(self.ops)
        rr = [self._region(r) for r in (list(reads) + [src])]
        ww = [self._region(w) for w in (list(writes) + [dst])]
        self.ops.append((queue, (dst, src), "d"))
        self.deps.append(self._record(opid, rr, ww))
        self.flag.append(True)
        self.dbg.append(tag)
        return opid

    def finalize(self):
        for opid, dep in enumerate(self.deps):
            for d in dep:
                if self.ops[d][2] == "d":
                    continue
                self.flag[d] = True
        self.inc_idx = [0] * len(self.ops)
        counters = {}
        for opid, (eng, _, kind) in enumerate(self.ops):
            if kind == "c" and self.flag[opid]:
                counters[eng] = counters.get(eng, 0) + 1
                self.inc_idx[opid] = counters[eng]
        self.dma_pos = {}
        self.wave_end = {}
        self.wave_start_fence = {}
        qpos = {}
        waves = {}
        for opid, (eng, _, kind) in enumerate(self.ops):
            if kind != "d":
                continue
            qpos[eng] = qpos.get(eng, 0) + 1
            pos = qpos[eng]
            self.dma_pos[opid] = pos
            newwave = bool(self.deps[opid]) or eng not in waves
            if newwave:
                waves.setdefault(eng, []).append([pos, pos])
                if len(waves[eng]) > 1:
                    self.wave_start_fence[opid] = 16 * waves[eng][-2][1]
            else:
                waves[eng][-1][1] = pos
        for opid, (eng, _, kind) in enumerate(self.ops):
            if kind != "d":
                continue
            pos = self.dma_pos[opid]
            for (s0, e0) in waves[eng]:
                if s0 <= pos <= e0:
                    self.wave_end[opid] = e0
                    break
        self.counters = dict(counters)
        for q, n in qpos.items():
            self.counters[q] = n

    def replay(self, myeng, eng, sems, qsems):
        last_wait = {}

        def do_wait(sem, val, key):
            if last_wait.get(key, -1) >= val:
                return
            eng.wait_ge(sem, val)
            last_wait[key] = val

        for opid, (e, payload, kind) in enumerate(self.ops):
            if e != myeng:
                continue
            if kind == "d" and opid in self.wave_start_fence:
                do_wait(qsems[myeng], self.wave_start_fence[opid], myeng)
            want = {}
            for d in self.deps[opid]:
                de = self.ops[d][0]
                dk = self.ops[d][2]
                if dk == "d":
                    if de == myeng and kind == "d":
                        continue
                    sem, val, key = qsems[de], 16 * self.wave_end[d], de
                else:
                    if not self.flag[d]:
                        raise RuntimeError(f"dep on unflagged op {d}")
                    sem, val, key = sems[de], self.inc_idx[d], de
                if want.get(key, (None, -1))[1] < val:
                    want[key] = (sem, val)
            for key, (sem, val) in want.items():
                do_wait(sem, val, key)
            if kind == "d":
                dst, src = payload
                ins = eng.dma_start(dst, src)
                ins.then_inc(qsems[myeng], 16)
            else:
                ins = payload(eng)
                if self.flag[opid] and ins is not None:
                    ins.then_inc(sems[myeng], 1)
                elif self.flag[opid] and ins is None:
                    raise RuntimeError(f"op {opid} {self.dbg[opid]} returned no ins")


def build_program(nsub=8):
    nc = bass.Bass("TRN2", target_bir_lowering=False)

    dt_in = {}

    def din(name, shape):
        h = nc.dram_tensor(name, list(shape), F32, kind="ExternalInput")
        dt_in[name] = h
        return h

    x_in = din("x", (T, D))
    y_out = nc.dram_tensor("y", [T, D], F32, kind="ExternalOutput")

    W = {}
    for name, shape in INPUT_SHAPES:
        W[name] = din(name, shape)

    ident_f = din("aux_ident_f", (128, 128))
    aux_bf = {}
    for name, shape in AUX_BF16_SHAPES:
        aux_bf[name] = nc.dram_tensor(name, list(shape), BF16, kind="ExternalInput")

    scr = {}
    _scr_kind = "ExternalOutput" if os.environ.get("ATHENA_DBG_SCR") else "Internal"
    def dscr(name, shape, dtype=BF16):
        h = nc.dram_tensor(name, list(shape), dtype, kind=_scr_kind)
        scr[name] = h
        return h

    dscr("s_inprojT", (D, 4096))
    dscr("s_outprojT", (2048, D))
    dscr("s_xwT", (512, 192))
    dscr("s_dtwT", (16, 2048))
    dscr("s_wgT", (D, 2816))
    dscr("s_wuT", (D, 2816))
    dscr("s_wdT", (2816, D))
    dscr("s_w1tT", (D, 4096))
    dscr("s_w2tT", (4096, D))
    dscr("s_lam", (64, 128), F32)

    P = Prog(nc)

    from contextlib import ExitStack
    with ExitStack() as stack:
        ent = stack.enter_context
        RESID = ent(nc.sbuf_tensor("RESID", [128, NTC * D], F32))
        NFM = ent(nc.sbuf_tensor("NFM", [128, NDC * T], BF16))
        BIGA = ent(nc.sbuf_tensor("BIGA", [128, 16384], BF16))
        BIGB = ent(nc.sbuf_tensor("BIGB", [128, 16384], BF16))
        F32S = ent(nc.sbuf_tensor("F32S", [128, 3072], F32))
        BF16S = ent(nc.sbuf_tensor("BF16S", [128, 8192], BF16))
        BIAS = ent(nc.sbuf_tensor("BIAS", [128, 4096], BF16))
        KVC = ent(nc.sbuf_tensor("KVC", [128, 4096], BF16))
        STRM = ent(nc.sbuf_tensor("STRM", [128, 1024], BF16))
        STAT = ent(nc.sbuf_tensor("STAT", [128, 128], F32))
        IDF = ent(nc.sbuf_tensor("IDF", [128, 128], F32))
        IDB = ent(nc.sbuf_tensor("IDB", [128, 128], BF16))
        ONEB = ent(nc.sbuf_tensor("ONEB", [128, 128], BF16))
        ONEF = ent(nc.sbuf_tensor("ONEF", [128, 8], F32))
        PS0 = ent(nc.psum_tensor("PS0", [128, 512], F32))
        PS1 = ent(nc.psum_tensor("PS1", [128, 512], F32))
        PS2 = ent(nc.psum_tensor("PS2", [128, 512], F32))
        PS3 = ent(nc.psum_tensor("PS3", [128, 512], F32))
        PS4 = ent(nc.psum_tensor("PS4", [128, 512], F32))
        PS5 = ent(nc.psum_tensor("PS5", [128, 512], F32))
        PB0 = ent(nc.psum_tensor("PB0", [128, 1024], BF16))
        PB1 = ent(nc.psum_tensor("PB1", [128, 1024], BF16))
        semT = ent(nc.semaphore("semT"))
        semV = ent(nc.semaphore("semV"))
        semS = ent(nc.semaphore("semS"))
        semG = ent(nc.semaphore("semG"))
        qY = ent(nc.semaphore("qY"))
        qG = ent(nc.semaphore("qG"))
        qS = ent(nc.semaphore("qS"))
        qV = ent(nc.semaphore("qV"))
        block = ent(nc.Block())
        PSF = [PS0, PS1, PS2, PS3, PS4, PS5]
        PSB = [PB0, PB1]
        ctx = dict(nc=nc, P=P, W=W, scr=scr, aux=aux_bf, x_in=x_in, y_out=y_out,
                   RESID=RESID, NFM=NFM, BIGA=BIGA, BIGB=BIGB, F32S=F32S,
                   BF16S=BF16S, BIAS=BIAS, KVC=KVC, STRM=STRM,
                   STAT=STAT, IDF=IDF, IDB=IDB, ONEB=ONEB, ONEF=ONEF,
                   PSF=PSF, PSB=PSB, ident_f=ident_f)

        _cc = {}
        def const_col(val):
            if val not in _cc:
                slot = 120 + len(_cc)
                assert slot < 128
                col = STAT[:, slot:slot + 1]
                P.op("V", lambda e, col=col, val=val: e.memset(col, val),
                     writes=[col], tag=f"const{val}")
                _cc[val] = col
            return _cc[val]
        ctx["const_col"] = const_col

        build_body(ctx, nsub)

        P.finalize()
        sems = {"T": semT, "V": semV, "S": semS, "G": semG}
        qsems = {"Y": qY, "G2": qG, "S2": qS, "V2": qV}

        @block.tensor
        def _(e):
            P.replay("T", e, sems, qsems)

        @block.vector
        def _(e):
            P.replay("V", e, sems, qsems)
            P.replay("V2", e, sems, qsems)

        @block.scalar
        def _(e):
            P.replay("S", e, sems, qsems)
            P.replay("S2", e, sems, qsems)

        @block.gpsimd
        def _(e):
            P.replay("G", e, sems, qsems)
            P.replay("G2", e, sems, qsems)

        @block.sync
        def _(e):
            P.replay("Y", e, sems, qsems)
            e.wait_ge(qY, 16 * P.counters.get("Y", 0))

    return nc


def build_body(c, nsub):
    P, nc = c["P"], c["nc"]
    RESID = c["RESID"]
    x_in, y_out = c["x_in"], c["y_out"]

    P.dma(c["IDF"][:, :], c["ident_f"].ap()[:, :], queue="Y", tag="idf")
    P.dma(c["IDB"][:, :], c["aux"]["aux_ident_b"].ap()[:, :], queue="Y", tag="idb")
    P.dma(c["ONEB"][:, :], c["aux"]["aux_ones_b"].ap()[:, :], queue="Y", tag="oneb")
    P.op("V", lambda e: e.memset(c["ONEF"][:, :], 1.0), writes=[c["ONEF"][:, :]],
         tag="onef")

    for tc in range(NTC):
        P.dma(RESID[:, tc * D:(tc + 1) * D], x_in.ap()[tc * 128:(tc + 1) * 128, :],
              queue="Y", tag=f"xin{tc}")

    if nsub >= 1:
        preprocess_weights(c)

    subs = [sub_l0_mamba, sub_l0_ffn, sub_l0_attnb, sub_l0_ffn,
            sub_l1_mla, sub_l1_ffn, sub_l1_gdn, sub_l1_ffn]
    for i in range(min(nsub, 8)):
        subs[i](c, i)

    for tc in range(NTC):
        P.dma(y_out.ap()[tc * 128:(tc + 1) * 128, :], RESID[:, tc * D:(tc + 1) * D],
              queue="Y", tag=f"yout{tc}")


def layernorm_and_transpose(c, tag, mode="plain", wscol=None, lnw=None, lnb=None):
    _lncut = int(os.environ.get("ATHENA_LNCUT", "99"))
    P, nc = c["P"], c["nc"]
    RESID, NFM, STAT, IDB = c["RESID"], c["NFM"], c["STAT"], c["IDB"]
    F32S, BF16S = c["F32S"], c["BF16S"]
    PSB = c["PSB"]
    for tc in range(NTC):
        r = RESID[:, tc * D:(tc + 1) * D]
        sb = (tc % 2) * 8
        sm = STAT[:, sb + 0:sb + 1]
        ssq = STAT[:, sb + 1:sb + 2]
        mean = STAT[:, sb + 2:sb + 3]
        rstd = STAT[:, sb + 3:sb + 4]
        scratch = c["F32S"][:, (tc % 2) * D:(tc % 2 + 1) * D]
        P.op("V", lambda e, r=r, sm=sm: e.reduce_sum(sm, r, axis=AXX),
             reads=[r], writes=[sm], tag=f"{tag}ln_sum{tc}")
        P.op("S", lambda e, r=r, scratch=scratch, ssq=ssq:
             e.activation(scratch, r, AF.Square, accum_out=ssq),
             reads=[r], writes=[scratch, ssq], tag=f"{tag}ln_sq{tc}")
        P.op("V", lambda e, sm=sm, mean=mean:
             e.tensor_scalar_mul(mean, sm, 1.0 / D),
             reads=[sm], writes=[mean], tag=f"{tag}ln_mean{tc}")
        v1 = STAT[:, sb + 4:sb + 5]
        P.op("V", lambda e, mean=mean, v1=v1:
             e.tensor_tensor(v1, mean, mean, ALU.mult),
             reads=[mean], writes=[v1], tag=f"{tag}ln_m2{tc}")
        v2 = STAT[:, sb + 5:sb + 6]
        P.op("V", lambda e, ssq=ssq, v1=v1, v2=v2:
             e.scalar_tensor_tensor(v2, ssq, 1.0 / D, v1, ALU.mult, ALU.subtract),
             reads=[ssq, v1], writes=[v2], tag=f"{tag}ln_var{tc}")
        eps = c["const_col"](1e-5)
        P.op("S", lambda e, v2=v2, v1=v1, eps=eps:
             e.activation(v1, v2, AF.Sqrt, bias=eps),
             reads=[v2, eps], writes=[v1], tag=f"{tag}ln_std{tc}")
        P.op("V", lambda e, v1=v1, rstd=rstd: e.reciprocal(rstd, v1),
             reads=[v1], writes=[rstd], tag=f"{tag}ln_rstd{tc}")
        if _lncut <= 1:
            continue
        if mode == "double":
            nf = F32S[:, 0:D]
            P.op("V", lambda e, r=r, mean=mean, rstd=rstd, nf=nf:
                 e.scalar_tensor_tensor(
                     nf, r, mean, rstd.broadcast_to([128, D]),
                     ALU.subtract, ALU.mult),
                 reads=[r, mean, rstd], writes=[nf], tag=f"{tag}ln_n1{tc}")
            sm2 = STAT[:, sb + 6:sb + 7]
            P.op("V", lambda e, sm2=sm2, nf=nf: e.reduce_sum(sm2, nf, axis=AXX),
                 reads=[nf], writes=[sm2], tag=f"{tag}l2s{tc}")
            sq2 = STAT[:, sb + 7:sb + 8]
            scratch2 = c["F32S"][:, 1024:2048]
            P.op("S", lambda e, nf=nf, scratch2=scratch2, sq2=sq2:
                 e.activation(scratch2, nf, AF.Square, accum_out=sq2),
                 reads=[nf], writes=[scratch2, sq2], tag=f"{tag}l2q{tc}")
            P.op("V", lambda e, sm2=sm2: e.tensor_scalar_mul(sm2, sm2, 1.0 / D),
                 reads=[sm2], writes=[sm2], tag=f"{tag}l2m{tc}")
            v3 = STAT[:, sb + 2:sb + 3]
            P.op("V", lambda e, v3=v3, sm2=sm2:
                 e.tensor_tensor(v3, sm2, sm2, ALU.mult),
                 reads=[sm2], writes=[v3], tag=f"{tag}l2m2{tc}")
            v4 = STAT[:, sb + 3:sb + 4]
            P.op("V", lambda e, v4=v4, sq2=sq2, v3=v3:
                 e.scalar_tensor_tensor(v4, sq2, 1.0 / D, v3, ALU.mult,
                                        ALU.subtract),
                 reads=[sq2, v3], writes=[v4], tag=f"{tag}l2v{tc}")
            eps = c["const_col"](1e-5)
            P.op("S", lambda e, v4=v4, v3=v3, eps=eps:
                 e.activation(v3, v4, AF.Sqrt, bias=eps),
                 reads=[v4, eps], writes=[v3], tag=f"{tag}l2d{tc}")
            rstd2 = STAT[:, sb + 4:sb + 5]
            P.op("V", lambda e, rstd2=rstd2, v3=v3: e.reciprocal(rstd2, v3),
                 reads=[v3], writes=[rstd2], tag=f"{tag}l2r{tc}")
            t5 = c["F32S"][:, 1024:2048]
            P.op("V", lambda e, t5=t5, nf=nf, sm2=sm2, rstd2=rstd2:
                 e.scalar_tensor_tensor(t5, nf, sm2,
                                        rstd2.broadcast_to([128, D]),
                                        ALU.subtract, ALU.mult),
                 reads=[nf, sm2, rstd2], writes=[t5], tag=f"{tag}l2n{tc}")
            P.op("V", lambda e, t5=t5, lnw=lnw: e.tensor_tensor(t5, t5, lnw,
                                                                ALU.mult),
                 reads=[t5, lnw], writes=[t5], tag=f"{tag}l2w{tc}")
            nbf = c["BF16S"][:, 0:D]
            P.op("V", lambda e, nbf=nbf, t5=t5, lnb=lnb:
                 e.tensor_tensor(nbf, t5, lnb, ALU.add),
                 reads=[t5, lnb], writes=[nbf], tag=f"{tag}l2b{tc}")
            P.op("V", lambda e, r=r, nf=nf: e.tensor_tensor(r, r, nf, ALU.add),
                 reads=[r, nf], writes=[r], tag=f"{tag}lnadd{tc}")
        elif mode == "plain":
            nbf = c["BF16S"][:, (tc % 2) * D:(tc % 2 + 1) * D]
            P.op("V", lambda e, r=r, mean=mean, rstd=rstd, nbf=nbf:
                 e.scalar_tensor_tensor(
                     nbf, r, mean, rstd.broadcast_to([128, D]),
                     ALU.subtract, ALU.mult),
                 reads=[r, mean, rstd], writes=[nbf], tag=f"{tag}ln_norm{tc}")
        else:
            nf = F32S[:, 0:D]
            P.op("V", lambda e, r=r, mean=mean, rstd=rstd, nf=nf:
                 e.scalar_tensor_tensor(
                     nf, r, mean, rstd.broadcast_to([128, D]),
                     ALU.subtract, ALU.mult),
                 reads=[r, mean, rstd], writes=[nf], tag=f"{tag}ln_norm{tc}")
            nbf = BF16S[:, 0:D]
            if mode == "fp4":
                fp4_quant_tile(c, nf, D, 0, 1024, 2048, 0, f"{tag}xq{tc}_")
            else:
                xs = STAT[:, sb + 6:sb + 7]
                P.op("V", lambda e, xs=xs, nf=nf:
                     e.reduce_max(xs, nf, axis=AXX, apply_absolute_value=True),
                     reads=[nf], writes=[xs], tag=f"{tag}xs{tc}")
                P.op("V", lambda e, xs=xs:
                     e.tensor_scalar_max(xs, xs, 1e-5),
                     reads=[xs], writes=[xs], tag=f"{tag}xsc{tc}")
                rcp = STAT[:, sb + 7:sb + 8]
                P.op("V", lambda e, rcp=rcp, xs=xs: e.reciprocal(rcp, xs),
                     reads=[xs], writes=[rcp], tag=f"{tag}xr{tc}")
                P.op("V", lambda e, rcp=rcp: e.tensor_scalar_mul(rcp, rcp, 127.0),
                     reads=[rcp], writes=[rcp], tag=f"{tag}xr2{tc}")
                dq = STAT[:, 16 + tc:17 + tc]
                P.op("V", lambda e, dq=dq, xs=xs, wscol=wscol:
                     e.tensor_tensor(dq, xs, wscol, ALU.mult),
                     reads=[xs, wscol], writes=[dq], tag=f"{tag}dq{tc}")
                P.op("V", lambda e, dq=dq: e.tensor_scalar_mul(dq, dq, 1.0 / 127.0),
                     reads=[dq], writes=[dq], tag=f"{tag}dq2{tc}")
                v = F32S[:, 1024:2048]
                m = F32S[:, 2048:3072]
                P.op("V", lambda e, v=v, nf=nf, rcp=rcp:
                     e.tensor_scalar_mul(v, nf, rcp),
                     reads=[nf, rcp], writes=[v], tag=f"{tag}rv{tc}")
                P.op("V", lambda e, m=m, v=v:
                     e.tensor_single_scalar(m, v, 12582912.0, ALU.add),
                     reads=[v], writes=[m], tag=f"{tag}rm{tc}")
                P.op("V", lambda e, nbf=nbf, m=m:
                     e.tensor_single_scalar(nbf, m, -12582912.0, ALU.add),
                     reads=[m], writes=[nbf], tag=f"{tag}rq{tc}")
        if _lncut <= 2:
            continue
        _trmode = int(os.environ.get("ATHENA_TRMODE", "0"))
        IDB = c["IDB"]
        for dcg in range(2):
            if _trmode == 1 and dcg == 1:
                continue
            pb = c["PSB"][dcg % 2] if _trmode != 3 else c["PSF"][dcg % 2]
            for j in range(4):
                dc = dcg * 4 + j
                blk = nbf[:, j * 128 + dcg * 512:(j + 1) * 128 + dcg * 512]
                P.op("T", lambda e, blk=blk, pb=pb, j=j:
                     e.transpose(pb[:, j * 128:(j + 1) * 128], blk, IDB[:, :]),
                     reads=[blk, IDB[:, :]],
                     writes=[pb[:, j * 128:(j + 1) * 128]], tag=f"{tag}ln_tr{tc}_{dc}")
            for j in range(4):
                if _trmode == 4:
                    continue
                dc = dcg * 4 + j
                dst = NFM[:, dc * T + tc * 128: dc * T + (tc + 1) * 128]
                _ev = "V" if _trmode == 2 else "S"
                P.op(_ev, lambda e, pb=pb, j=j, dst=dst:
                     e.copy(dst, pb[:, j * 128:(j + 1) * 128]) if _ev == "S"
                     else e.tensor_copy(dst, pb[:, j * 128:(j + 1) * 128]),
                     reads=[pb[:, j * 128:(j + 1) * 128]], writes=[dst],
                     tag=f"{tag}ln_ev{tc}_{dc}")


def sub_l0_mamba(c, idx):
    tg = f"s{idx}"
    _mcut = int(os.environ.get("ATHENA_MCUT", "99"))
    layernorm_and_transpose(c, tg)
    if _mcut <= 1:
        return
    P, W, scr = c["P"], c["W"], c["scr"]
    NFM, BIGA, BIGB = c["NFM"], c["BIGA"], c["BIGB"]
    F32S, BF16S, BIAS, KVC, STRM, STAT = (c["F32S"], c["BF16S"], c["BIAS"],
                                          c["KVC"], c["STRM"], c["STAT"])
    RESID, IDF = c["RESID"], c["IDF"]
    PSF, PSB = c["PSF"], c["PSB"]
    one_col = c["const_col"](1.0)

    SEL = KVC[64:80, 1024:3072]
    P.dma(SEL, c["aux"]["aux_sel16"].ap()[:, :], queue="Y", tag=f"{tg}sel")
    BB = BIAS[64:80, 0:2048]
    CC = BIAS[64:80, 2048:4096]
    DTR = KVC[32:48, 1024:3072]

    ps_rot = [0]
    def bank():
        ps_rot[0] = (ps_rot[0] + 1) % 4
        return PSF[ps_rot[0]]

    for half in range(2):
        for k in range(16):
            jglob = (half * 8 + k) if k < 8 else (16 + half * 8 + (k - 8))
            wbuf = STRM[:, 0:1024] if (k % 2 == 0) else KVC[:, 0:1024]
            src = scr["s_inprojT"].ap()[:, jglob * 128:(jglob + 1) * 128]
            src = src.rearrange("(dc p) r -> p dc r", p=128)
            dstb = wbuf.rearrange("p (dc r) -> p dc r", r=128)
            P.dma(dstb, src, queue="Y", tag=f"{tg}ipw{half}_{k}")
            for tc in range(4):
                ps = bank()
                for dc in range(8):
                    P.op("T", lambda e, ps=ps, wbuf=wbuf, dc=dc, tc=tc:
                         e.matmul(ps[:, 0:512], wbuf[:, dc * 128:(dc + 1) * 128],
                                  NFM[:, dc * T + tc * 512:dc * T + (tc + 1) * 512],
                                  start=(dc == 0), stop=(dc == 7)),
                         reads=[wbuf[:, dc * 128:(dc + 1) * 128],
                                NFM[:, dc * T + tc * 512:dc * T + (tc + 1) * 512]] +
                               ([ps[:, 0:512]] if dc else []),
                         writes=[ps[:, 0:512]], tag=f"{tg}ipm{half}_{k}_{tc}_{dc}")
                if k < 8:
                    dst = F32S[:, tc * 512:(tc + 1) * 512]
                    P.op("V", lambda e, ps=ps, dst=dst: e.tensor_copy(dst, ps[:, 0:512]),
                         reads=[ps[:, 0:512]], writes=[dst], tag=f"{tg}ipx{half}_{k}_{tc}")
                else:
                    dst = BIGB[:, (k - 8) * 2048 + tc * 512:(k - 8) * 2048 + (tc + 1) * 512]
                    P.op("S", lambda e, ps=ps, dst=dst: e.copy(dst, ps[:, 0:512]),
                         reads=[ps[:, 0:512]], writes=[dst], tag=f"{tg}ipz{half}_{k}_{tc}")
            if k < 8:
                ch = half * 8 + k
                cw = F32S[:, 2560:2564]
                cb = F32S[:, 2564:2565]
                P.dma(cw, W["L0_attn_a_conv_w"].ap()[ch * 128:(ch + 1) * 128, 0, :],
                      queue="Y", tag=f"{tg}cw{half}_{k}")
                P.dma(cb, W["L0_attn_a_conv_b"].ap()[ch * 128:(ch + 1) * 128].unsqueeze(1),
                      queue="Y", tag=f"{tg}cb{half}_{k}")
                x_t = F32S[:, 0:2048]
                acc = F32S[:, 2048:2560]
                for tc in range(4):
                    a = acc[:, 0:512]
                    xs = x_t[:, tc * 512:(tc + 1) * 512]
                    P.op("V", lambda e, a=a, xs=xs, cw=cw, cb=cb:
                         e.tensor_scalar(a, xs, cw[:, 3:4], cb, ALU.mult, ALU.add),
                         reads=[xs, cw, cb], writes=[a], tag=f"{tg}cv3_{half}_{k}_{tc}")
                    for kk in range(3):
                        sh = 3 - kk
                        lo = tc * 512 - sh
                        if lo < 0:
                            aa = a[:, sh:512]
                            xx = x_t[:, 0:512 - sh]
                        else:
                            aa = a[:, 0:512]
                            xx = x_t[:, lo:lo + 512]
                        P.op("V", lambda e, aa=aa, xx=xx, cw=cw, kk=kk:
                             e.scalar_tensor_tensor(aa, xx, cw[:, kk:kk + 1], aa,
                                                    ALU.mult, ALU.add),
                             reads=[xx, cw, aa], writes=[aa],
                             tag=f"{tg}cv{kk}_{half}_{k}_{tc}")
                    dstu = BIGA[:, k * 2048 + tc * 512:k * 2048 + (tc + 1) * 512]
                    P.op("S", lambda e, a=a, dstu=dstu:
                         e.activation(dstu, a, AF.Silu),
                         reads=[a], writes=[dstu], tag=f"{tg}cvs{half}_{k}_{tc}")

        if _mcut <= 2:
            continue
        for hl in range(2):
            h = half * 2 + hl
            xwt = STRM[:, 512:704]
            src = scr["s_xwT"].ap()[:, h * 48:(h + 1) * 48]
            src = src.rearrange("(kc p) r -> p kc r", p=128)
            P.dma(xwt.rearrange("p (kc r) -> p kc r", r=48), src, queue="Y",
                  tag=f"{tg}xw{h}")
            for tc in range(4):
                pss = [PSF[4], PSF[5], bank()]
                for si in range(3):
                    for kc in range(4):
                        u_sl = BIGA[:, (hl * 4 + kc) * 2048 + tc * 512:
                                    (hl * 4 + kc) * 2048 + (tc + 1) * 512]
                        lh = xwt[:, kc * 48 + si * 16:kc * 48 + si * 16 + 16]
                        P.op("T", lambda e, ps=pss[si], lh=lh, u_sl=u_sl, kc=kc:
                             e.matmul(ps[0:16, 0:512], lh, u_sl,
                                      start=(kc == 0), stop=(kc == 3)),
                             reads=[lh, u_sl] + ([pss[si][0:16, 0:512]] if kc else []),
                             writes=[pss[si][0:16, 0:512]],
                             tag=f"{tg}xd{h}_{tc}_{si}_{kc}")
                for si, dstt in ((0, DTR), (1, BB), (2, CC)):
                    stg = BF16S[0:16, si * 512:(si + 1) * 512]
                    P.op("V", lambda e, stg=stg, ps=pss[si]:
                         e.tensor_copy(stg, ps[0:16, 0:512]),
                         reads=[pss[si][0:16, 0:512]], writes=[stg],
                         tag=f"{tg}xde{h}_{tc}_{si}")
                    P.dma(dstt[:, tc * 512:(tc + 1) * 512], stg, queue="G2",
                          tag=f"{tg}xdd{h}_{tc}_{si}")
            if _mcut <= 3:
                continue
            dtw = STRM[32:48, 0:512]
            P.dma(dtw, scr["s_dtwT"].ap()[:, h * 512:(h + 1) * 512], queue="Y",
                  tag=f"{tg}dtw{h}")
            for dtile in range(4):
                krow = hl * 4 + dtile
                chg = h * 4 + dtile
                dtb = STAT[:, 64:65]
                dsk = STAT[:, 65:66]
                alog = STAT[:, 66:82]
                aneg = STAT[:, 82:98]
                P.dma(dtb, W["L0_attn_a_dt_proj_b"].ap()[chg * 128:(chg + 1) * 128].unsqueeze(1),
                      queue="Y", tag=f"{tg}dtb{h}_{dtile}")
                P.dma(dsk, W["L0_attn_a_Dskip"].ap()[chg * 128:(chg + 1) * 128].unsqueeze(1),
                      queue="Y", tag=f"{tg}dsk{h}_{dtile}")
                P.dma(alog, W["L0_attn_a_A_log"].ap()[chg * 128:(chg + 1) * 128, :],
                      queue="Y", tag=f"{tg}alog{h}_{dtile}")
                P.op("S", lambda e, alog=alog, aneg=aneg:
                     e.activation(aneg, alog, AF.Exp),
                     reads=[alog], writes=[aneg], tag=f"{tg}aexp{h}_{dtile}")
                P.op("V", lambda e, aneg=aneg: e.tensor_scalar_mul(aneg, aneg, -1.0),
                     reads=[aneg], writes=[aneg], tag=f"{tg}aneg{h}_{dtile}")
                for th in range(2):
                    delta = F32S[:, 0:1024]
                    dAs = F32S[:, 1024:2048]
                    s_t = F32S[:, 2048:3072]
                    du = BF16S[:, 0:1024]
                    dbu = BF16S[:, 1024:2048]
                    zs = BF16S[:, 2048:3072]
                    u_sl = BIGA[:, krow * 2048 + th * 1024:krow * 2048 + (th + 1) * 1024]
                    for sc in range(2):
                        tcg = th * 2 + sc
                        ps = bank()
                        P.op("T", lambda e, ps=ps, dtile=dtile, tcg=tcg:
                             e.matmul(ps[0:128, 0:512],
                                      dtw[:, dtile * 128:(dtile + 1) * 128],
                                      DTR[:, tcg * 512:(tcg + 1) * 512],
                                      start=True, stop=True),
                             reads=[dtw[:, dtile * 128:(dtile + 1) * 128],
                                    DTR[:, tcg * 512:(tcg + 1) * 512]],
                             writes=[ps[:, 0:512]], tag=f"{tg}dmm{h}_{dtile}_{tcg}")
                        va = F32S[:, 2048 + sc * 512:2048 + (sc + 1) * 512]
                        dsl = delta[:, sc * 512:(sc + 1) * 512]
                        P.op("S", lambda e, va=va, ps=ps, dtb=dtb:
                             e.activation(va, ps[:, 0:512], AF.Abs, bias=dtb),
                             reads=[ps[:, 0:512], dtb], writes=[va],
                             tag=f"{tg}sp1_{h}_{dtile}_{tcg}")
                        P.op("S", lambda e, va=va:
                             e.activation(va, va, AF.Exp, scale=-1.0),
                             reads=[va], writes=[va], tag=f"{tg}sp2_{h}_{dtile}_{tcg}")
                        P.op("S", lambda e, va=va, one_col=one_col:
                             e.activation(va, va, AF.Ln, bias=one_col),
                             reads=[va, one_col], writes=[va],
                             tag=f"{tg}sp3_{h}_{dtile}_{tcg}")
                        P.op("S", lambda e, dsl=dsl, ps=ps, dtb=dtb:
                             e.activation(dsl, ps[:, 0:512], AF.Relu, bias=dtb),
                             reads=[ps[:, 0:512], dtb], writes=[dsl],
                             tag=f"{tg}sp4_{h}_{dtile}_{tcg}")
                        P.op("V", lambda e, dsl=dsl, va=va:
                             e.tensor_tensor(dsl, dsl, va, ALU.add),
                             reads=[dsl, va], writes=[dsl],
                             tag=f"{tg}sp5_{h}_{dtile}_{tcg}")
                    P.op("V", lambda e, du=du, delta=delta, u_sl=u_sl:
                         e.tensor_tensor(du, delta, u_sl, ALU.mult),
                         reads=[delta, u_sl], writes=[du], tag=f"{tg}du{h}_{dtile}_{th}")
                    yb = [PSF[4], PSF[5]]
                    for sc in range(2):
                        P.op("V", lambda e, sc=sc, yb=yb: e.memset(yb[sc][:, 0:512], 0.0),
                             writes=[yb[sc][:, 0:512]], tag=f"{tg}yz{h}_{dtile}_{th}_{sc}")
                    for n in range(16):
                        an = aneg[:, n:n + 1]
                        P.op("S", lambda e, dAs=dAs, delta=delta, an=an:
                             e.activation(dAs, delta, AF.Exp, scale=an),
                             reads=[delta, an], writes=[dAs],
                             tag=f"{tg}dA{h}_{dtile}_{th}_{n}")
                        for sc in range(2):
                            tcg = th * 2 + sc
                            psb = bank()
                            P.op("T", lambda e, psb=psb, n=n, tcg=tcg:
                                 e.matmul(psb[:, 0:512],
                                          SEL[:, n * 128:(n + 1) * 128],
                                          BB[:, tcg * 512:(tcg + 1) * 512],
                                          start=True, stop=True),
                                 reads=[SEL[:, n * 128:(n + 1) * 128],
                                        BB[:, tcg * 512:(tcg + 1) * 512]],
                                 writes=[psb[:, 0:512]],
                                 tag=f"{tg}bB{h}_{dtile}_{th}_{n}_{sc}")
                            P.op("V", lambda e, dbu=dbu, sc=sc, du=du, psb=psb:
                                 e.tensor_tensor(dbu[:, sc * 512:(sc + 1) * 512],
                                                 du[:, sc * 512:(sc + 1) * 512],
                                                 psb[:, 0:512], ALU.mult),
                                 reads=[du[:, sc * 512:(sc + 1) * 512], psb[:, 0:512]],
                                 writes=[dbu[:, sc * 512:(sc + 1) * 512]],
                                 tag=f"{tg}dbu{h}_{dtile}_{th}_{n}_{sc}")
                        carry = STAT[:, 48 + n:49 + n]
                        init = 0.0 if th == 0 else carry
                        rd = [dAs, dbu] + ([carry] if th == 1 else [])
                        P.op("V", lambda e, s_t=s_t, dAs=dAs, dbu=dbu, init=init:
                             e.tensor_tensor_scan(s_t, dAs, dbu, init,
                                                  ALU.mult, ALU.add),
                             reads=rd, writes=[s_t], tag=f"{tg}scan{h}_{dtile}_{th}_{n}")
                        if th == 0:
                            P.op("V", lambda e, carry=carry, s_t=s_t:
                                 e.tensor_copy(carry, s_t[:, 1023:1024]),
                                 reads=[s_t[:, 1023:1024]], writes=[carry],
                                 tag=f"{tg}carry{h}_{dtile}_{n}")
                        for sc in range(2):
                            tcg = th * 2 + sc
                            psc = bank()
                            P.op("T", lambda e, psc=psc, n=n, tcg=tcg:
                                 e.matmul(psc[:, 0:512],
                                          SEL[:, n * 128:(n + 1) * 128],
                                          CC[:, tcg * 512:(tcg + 1) * 512],
                                          start=True, stop=True),
                                 reads=[SEL[:, n * 128:(n + 1) * 128],
                                        CC[:, tcg * 512:(tcg + 1) * 512]],
                                 writes=[psc[:, 0:512]],
                                 tag=f"{tg}bC{h}_{dtile}_{th}_{n}_{sc}")
                            tmp = dAs[:, sc * 512:(sc + 1) * 512]
                            P.op("V", lambda e, tmp=tmp, s_t=s_t, sc=sc, psc=psc:
                                 e.tensor_tensor(tmp, s_t[:, sc * 512:(sc + 1) * 512],
                                                 psc[:, 0:512], ALU.mult),
                                 reads=[s_t[:, sc * 512:(sc + 1) * 512], psc[:, 0:512]],
                                 writes=[tmp], tag=f"{tg}yc{h}_{dtile}_{th}_{n}_{sc}")
                            P.op("V", lambda e, yb=yb, sc=sc, tmp=tmp:
                                 e.tensor_tensor(yb[sc][:, 0:512], yb[sc][:, 0:512],
                                                 tmp, ALU.add),
                                 reads=[yb[sc][:, 0:512], tmp],
                                 writes=[yb[sc][:, 0:512]],
                                 tag=f"{tg}ya{h}_{dtile}_{th}_{n}_{sc}")
                    z_sl = BIGB[:, krow * 2048 + th * 1024:krow * 2048 + (th + 1) * 1024]
                    P.op("S", lambda e, zs=zs, z_sl=z_sl:
                         e.activation(zs, z_sl, AF.Silu),
                         reads=[z_sl], writes=[zs], tag=f"{tg}zs{h}_{dtile}_{th}")
                    for sc in range(2):
                        t1 = F32S[:, 1024 + sc * 512:1024 + (sc + 1) * 512]
                        P.op("V", lambda e, t1=t1, u_sl=u_sl, sc=sc, dsk=dsk, yb=yb:
                             e.scalar_tensor_tensor(t1, u_sl[:, sc * 512:(sc + 1) * 512],
                                                    dsk, yb[sc][:, 0:512],
                                                    ALU.mult, ALU.add),
                             reads=[u_sl[:, sc * 512:(sc + 1) * 512], dsk,
                                    yb[sc][:, 0:512]],
                             writes=[t1], tag=f"{tg}yd{h}_{dtile}_{th}_{sc}")
                        yg = u_sl[:, sc * 512:(sc + 1) * 512]
                        P.op("V", lambda e, yg=yg, t1=t1, zs=zs, sc=sc:
                             e.tensor_tensor(yg, t1, zs[:, sc * 512:(sc + 1) * 512],
                                             ALU.mult),
                             reads=[t1, zs[:, sc * 512:(sc + 1) * 512]],
                             writes=[yg], tag=f"{tg}yg{h}_{dtile}_{th}_{sc}")

        for dmc in range(8):
            wbuf2 = STRM[:, 0:1024] if (dmc % 2 == 0) else KVC[:, 0:1024]
            src = scr["s_outprojT"].ap()[half * 1024:(half + 1) * 1024,
                                         dmc * 128:(dmc + 1) * 128]
            src = src.rearrange("(kc p) r -> p kc r", p=128)
            P.dma(wbuf2.rearrange("p (kc r) -> p kc r", r=128), src, queue="Y",
                  tag=f"{tg}opw{half}_{dmc}")
            for tc in range(4):
                ps = bank()
                for kc in range(8):
                    yg_sl = BIGA[:, kc * 2048 + tc * 512:kc * 2048 + (tc + 1) * 512]
                    P.op("T", lambda e, ps=ps, wbuf2=wbuf2, kc=kc, yg_sl=yg_sl:
                         e.matmul(ps[:, 0:512], wbuf2[:, kc * 128:(kc + 1) * 128],
                                  yg_sl, start=(kc == 0), stop=(kc == 7)),
                         reads=[wbuf2[:, kc * 128:(kc + 1) * 128], yg_sl] +
                               ([ps[:, 0:512]] if kc else []),
                         writes=[ps[:, 0:512]], tag=f"{tg}opm{half}_{dmc}_{tc}_{kc}")
                ev = F32S[:, 0:512]
                P.op("V", lambda e, ev=ev, ps=ps: e.tensor_copy(ev, ps[:, 0:512]),
                     reads=[ps[:, 0:512]], writes=[ev], tag=f"{tg}ope{half}_{dmc}_{tc}")
                for j in range(4):
                    pt = PSF[(ps_rot[0] + 1 + j % 2) % 4]
                    blk = ev[:, j * 128:(j + 1) * 128]
                    P.op("T", lambda e, pt=pt, blk=blk:
                         e.transpose(pt[:, 0:128], blk, IDF[:, :]),
                         reads=[blk, IDF[:, :]], writes=[pt[:, 0:128]],
                         tag=f"{tg}opt{half}_{dmc}_{tc}_{j}")
                    tcg = tc * 4 + j
                    rsl = RESID[:, tcg * D + dmc * 128:tcg * D + (dmc + 1) * 128]
                    P.op("V", lambda e, rsl=rsl, pt=pt:
                         e.tensor_tensor(rsl, rsl, pt[:, 0:128], ALU.add),
                         reads=[rsl, pt[:, 0:128]], writes=[rsl],
                         tag=f"{tg}opr{half}_{dmc}_{tc}_{j}")


def sub_l0_ffn(c, idx):
    tg = f"s{idx}"
    P, scr = c["P"], c["scr"]
    NFM, BIGA, F32S, BF16S, KVC = (c["NFM"], c["BIGA"], c["F32S"], c["BF16S"],
                                   c["KVC"])
    RESID, PSF, PSB, IDB = c["RESID"], c["PSF"], c["PSB"], c["IDB"]
    layernorm_and_transpose(c, tg, mode="fp4")
    nsl = [(i * 512, 512) for i in range(5)] + [(2560, 256)]
    for ns, (n0, nw) in enumerate(nsl):
        wg = BIGA[:, 0:8 * nw]
        wu = BIGA[:, 4096:4096 + 8 * nw]
        wd = BIGA[:, 8192:8192 + (nw // 128) * 1024]
        for wbuf, name in ((wg, "s_wgT"), (wu, "s_wuT")):
            src = scr[name].ap()[:, n0:n0 + nw].rearrange("(dc p) n -> p dc n", p=128)
            P.dma(wbuf.rearrange("p (dc n) -> p dc n", n=nw), src, queue="Y",
                  tag=f"{tg}w{name}{ns}")
        src = scr["s_wdT"].ap()[n0:n0 + nw, :].rearrange("(j p) m -> p j m", p=128)
        P.dma(wd.rearrange("p (j m) -> p j m", m=1024), src, queue="Y",
              tag=f"{tg}wd{ns}")
        for tc in range(NTC):
            psG, psU = PSF[4], PSF[5]
            for si, (ps, wbuf) in enumerate(((psG, wg), (psU, wu))):
                for dc in range(8):
                    lh = NFM[:, dc * T + tc * 128:dc * T + (tc + 1) * 128]
                    rh = wbuf[:, dc * nw:(dc + 1) * nw]
                    P.op("T", lambda e, ps=ps, lh=lh, rh=rh, dc=dc, nw=nw:
                         e.matmul(ps[:, 0:nw], lh, rh,
                                  start=(dc == 0), stop=(dc == 7)),
                         reads=[lh, rh] + ([ps[:, 0:nw]] if dc else []),
                         writes=[ps[:, 0:nw]], tag=f"{tg}gu{ns}_{tc}_{si}_{dc}")
            sg = F32S[:, 0:nw]
            P.op("S", lambda e, sg=sg, psG=psG, nw=nw:
                 e.activation(sg, psG[:, 0:nw], AF.Silu),
                 reads=[psG[:, 0:nw]], writes=[sg], tag=f"{tg}sg{ns}_{tc}")
            P.op("V", lambda e, sg=sg, psU=psU, nw=nw:
                 e.tensor_tensor(sg, sg, psU[:, 0:nw], ALU.mult),
                 reads=[sg, psU[:, 0:nw]], writes=[sg], tag=f"{tg}h{ns}_{tc}")
            hq = fp4_quant_tile(c, sg, nw, 0, 1024, 2048, 0, f"{tg}hq{ns}_{tc}_")
            nb = nw // 128
            pb = PSB[tc % 2]
            for j in range(nb):
                blk = hq[:, j * 128:(j + 1) * 128]
                P.op("T", lambda e, pb=pb, j=j, blk=blk:
                     e.transpose(pb[:, j * 128:(j + 1) * 128], blk, IDB[:, :]),
                     reads=[blk, IDB[:, :]], writes=[pb[:, j * 128:(j + 1) * 128]],
                     tag=f"{tg}ht{ns}_{tc}_{j}")
            hqf = BF16S[:, 1024:1024 + nb * 128]
            P.op("V", lambda e, hqf=hqf, pb=pb, nb=nb:
                 e.tensor_copy(hqf, pb[:, 0:nb * 128]),
                 reads=[pb[:, 0:nb * 128]], writes=[hqf], tag=f"{tg}he{ns}_{tc}")
            for dmh in range(2):
                psY = PSF[dmh]
                for j in range(nb):
                    lh = hqf[:, j * 128:(j + 1) * 128]
                    rh = wd[:, j * 1024 + dmh * 512:j * 1024 + dmh * 512 + 512]
                    P.op("T", lambda e, psY=psY, lh=lh, rh=rh, j=j, nb=nb:
                         e.matmul(psY[:, 0:512], lh, rh,
                                  start=(j == 0), stop=(j == nb - 1)),
                         reads=[lh, rh] + ([psY[:, 0:512]] if j else []),
                         writes=[psY[:, 0:512]], tag=f"{tg}dn{ns}_{tc}_{dmh}_{j}")
                rsl = RESID[:, tc * D + dmh * 512:tc * D + dmh * 512 + 512]
                P.op("V", lambda e, rsl=rsl, psY=psY:
                     e.tensor_tensor(rsl, rsl, psY[:, 0:512], ALU.add),
                     reads=[rsl, psY[:, 0:512]], writes=[rsl],
                     tag=f"{tg}ra{ns}_{tc}_{dmh}")


def sub_l0_attnb(c, idx):
    tg = f"s{idx}"
    P, W = c["P"], c["W"]
    NFM, BIGA, BIGB, F32S, BF16S, KVC, STRM = (c["NFM"], c["BIGA"], c["BIGB"],
                                               c["F32S"], c["BF16S"], c["KVC"],
                                               c["STRM"])
    RESID, PSF, IDF, STAT = c["RESID"], c["PSF"], c["IDF"], c["STAT"]
    layernorm_and_transpose(c, tg, mode="plain")
    HT = 1024
    for th2 in range(2):
        tcs = [th2 * 2, th2 * 2 + 1]
        for pa in range(16):
            wbuf = STRM[:, 0:1024] if pa % 2 == 0 else KVC[:, 0:1024]
            src = W["L0_attn_b_W_U_flat"].ap()[:, pa * 128:(pa + 1) * 128]
            src = src.rearrange("(dc p) r -> p dc r", p=128)
            P.dma(wbuf.rearrange("p (dc r) -> p dc r", r=128), src, queue="G2",
                  tag=f"{tg}wu{th2}_{pa}")
            for tl, tc in enumerate(tcs):
                ps = PSF[tl % 2]
                for dc in range(8):
                    P.op("T", lambda e, ps=ps, wbuf=wbuf, dc=dc, tc=tc:
                         e.matmul(ps[:, 0:512], wbuf[:, dc * 128:(dc + 1) * 128],
                                  NFM[:, dc * T + tc * 512:dc * T + (tc + 1) * 512],
                                  start=(dc == 0), stop=(dc == 7)),
                         reads=[wbuf[:, dc * 128:(dc + 1) * 128],
                                NFM[:, dc * T + tc * 512:dc * T + (tc + 1) * 512]] +
                               ([ps[:, 0:512]] if dc else []),
                         writes=[ps[:, 0:512]], tag=f"{tg}u2m{th2}_{pa}_{tc}_{dc}")
                dst = BIGA[:, pa * 1024 + tl * 512:pa * 1024 + (tl + 1) * 512]
                P.op("S", lambda e, dst=dst, ps=ps: e.copy(dst, ps[:, 0:512]),
                     reads=[ps[:, 0:512]], writes=[dst], tag=f"{tg}u2e{th2}_{pa}_{tl}")
        for pc in range(4):
            gt = BIGB[:, 12288:14336]
            src = W["L0_attn_b_G"].ap()[:, pc * 128:(pc + 1) * 128]
            src = src.rearrange("(kc p) r -> p kc r", p=128)
            P.dma(gt.rearrange("p (kc r) -> p kc r", r=128), src, queue="G2",
                  tag=f"{tg}gw{th2}_{pc}")
            wp = STRM[:, 0:1024]
            src = W["L0_attn_b_W_toP"].ap()[:, pc * 128:(pc + 1) * 128]
            src = src.rearrange("(dc p) r -> p dc r", p=128)
            P.dma(wp.rearrange("p (dc r) -> p dc r", r=128), src, queue="G2",
                  tag=f"{tg}pw{th2}_{pc}")
            wf = BIGB[:, 14336:15360]
            src = W["L0_attn_b_W_F"].ap()[:, pc * 128:(pc + 1) * 128]
            src = src.rearrange("(dc p) r -> p dc r", p=128)
            P.dma(wf.rearrange("p (dc r) -> p dc r", r=128), src, queue="G2",
                  tag=f"{tg}fw{th2}_{pc}")
            for tl, tc in enumerate(tcs):
                ps = PSF[tl % 2]
                nmm = 24
                mi = 0
                for dc in range(8):
                    P.op("T", lambda e, ps=ps, wp=wp, dc=dc, tc=tc, mi=mi:
                         e.matmul(ps[:, 0:512], wp[:, dc * 128:(dc + 1) * 128],
                                  NFM[:, dc * T + tc * 512:dc * T + (tc + 1) * 512],
                                  start=(mi == 0), stop=False),
                         reads=[wp[:, dc * 128:(dc + 1) * 128],
                                NFM[:, dc * T + tc * 512:dc * T + (tc + 1) * 512]] +
                               ([ps[:, 0:512]] if mi else []),
                         writes=[ps[:, 0:512]], tag=f"{tg}tm{th2}_{pc}_{tc}_{dc}")
                    mi += 1
                for kc in range(16):
                    u2sl = BIGA[:, kc * 1024 + tl * 512:kc * 1024 + (tl + 1) * 512]
                    P.op("T", lambda e, ps=ps, gt=gt, kc=kc, u2sl=u2sl, mi=mi:
                         e.matmul(ps[:, 0:512], gt[:, kc * 128:(kc + 1) * 128],
                                  u2sl, start=False, stop=(mi == nmm - 1)),
                         reads=[gt[:, kc * 128:(kc + 1) * 128], u2sl, ps[:, 0:512]],
                         writes=[ps[:, 0:512]], tag=f"{tg}tg{th2}_{pc}_{tc}_{kc}")
                    mi += 1
                dst = BIGB[:, 4096 + pc * 1024 + tl * 512:4096 + pc * 1024 + (tl + 1) * 512]
                P.op("S", lambda e, dst=dst, ps=ps: e.copy(dst, ps[:, 0:512]),
                     reads=[ps[:, 0:512]], writes=[dst], tag=f"{tg}te{th2}_{pc}_{tl}")
                ps2 = PSF[2 + tl % 2]
                for dc in range(8):
                    P.op("T", lambda e, ps2=ps2, wf=wf, dc=dc, tc=tc:
                         e.matmul(ps2[:, 0:512], wf[:, dc * 128:(dc + 1) * 128],
                                  NFM[:, dc * T + tc * 512:dc * T + (tc + 1) * 512],
                                  start=(dc == 0), stop=(dc == 7)),
                         reads=[wf[:, dc * 128:(dc + 1) * 128],
                                NFM[:, dc * T + tc * 512:dc * T + (tc + 1) * 512]] +
                               ([ps2[:, 0:512]] if dc else []),
                         writes=[ps2[:, 0:512]], tag=f"{tg}fm{th2}_{pc}_{tc}_{dc}")
                dst2 = BIGB[:, pc * 1024 + tl * 512:pc * 1024 + (tl + 1) * 512]
                P.op("S", lambda e, dst2=dst2, ps2=ps2:
                     e.activation(dst2, ps2[:, 0:512], AF.Sigmoid),
                     reads=[ps2[:, 0:512]], writes=[dst2], tag=f"{tg}fe{th2}_{pc}_{tl}")
        wl = STRM[:, 0:256]
        src = W["L0_attn_b_W_lam"].ap().rearrange("(pc p) r -> p pc r", p=128)
        P.dma(wl.rearrange("p (pc r) -> p pc r", r=64), src, queue="G2",
              tag=f"{tg}wl{th2}")
        wb = STRM[:, 256:512]
        src = W["L0_attn_b_B_mat"].ap().rearrange("(pc p) r -> p pc r", p=128)
        P.dma(wb.rearrange("p (pc r) -> p pc r", r=64), src, queue="G2",
              tag=f"{tg}wb{th2}")
        lam = F32S[0:64, 0:1024]
        uin = F32S[0:64, 1024:2048]
        for tl in range(2):
            ps = PSF[0]
            ps2 = PSF[1]
            for pc in range(4):
                fg_sl = BIGB[:, pc * 1024 + tl * 512:pc * 1024 + (tl + 1) * 512]
                t_sl = BIGB[:, 4096 + pc * 1024 + tl * 512:4096 + pc * 1024 + (tl + 1) * 512]
                P.op("T", lambda e, ps=ps, wl=wl, pc=pc, fg_sl=fg_sl:
                     e.matmul(ps[0:64, 0:512], wl[:, pc * 64:(pc + 1) * 64], fg_sl,
                              start=(pc == 0), stop=(pc == 3)),
                     reads=[wl[:, pc * 64:(pc + 1) * 64], fg_sl] +
                           ([ps[0:64, 0:512]] if pc else []),
                     writes=[ps[0:64, 0:512]], tag=f"{tg}lm{th2}_{tl}_{pc}")
                P.op("T", lambda e, ps2=ps2, wb=wb, pc=pc, t_sl=t_sl:
                     e.matmul(ps2[0:64, 0:512], wb[:, pc * 64:(pc + 1) * 64], t_sl,
                              start=(pc == 0), stop=(pc == 3)),
                     reads=[wb[:, pc * 64:(pc + 1) * 64], t_sl] +
                           ([ps2[0:64, 0:512]] if pc else []),
                     writes=[ps2[0:64, 0:512]], tag=f"{tg}um{th2}_{tl}_{pc}")
            dl = lam[:, tl * 512:(tl + 1) * 512]
            P.op("S", lambda e, dl=dl, ps=ps:
                 e.activation(dl, ps[0:64, 0:512], AF.Sigmoid),
                 reads=[ps[0:64, 0:512]], writes=[dl], tag=f"{tg}le{th2}_{tl}")
            du2 = uin[:, tl * 512:(tl + 1) * 512]
            P.op("V", lambda e, du2=du2, ps2=ps2:
                 e.tensor_copy(du2, ps2[0:64, 0:512]),
                 reads=[ps2[0:64, 0:512]], writes=[du2], tag=f"{tg}ue{th2}_{tl}")
        carry = STAT[0:64, 32:33]
        init = 0.0 if th2 == 0 else carry
        rd = [lam, uin] + ([carry] if th2 == 1 else [])
        P.op("V", lambda e, lam=lam, uin=uin, init=init:
             e.tensor_tensor_scan(lam, lam, uin, init, ALU.mult, ALU.add),
             reads=rd, writes=[lam], tag=f"{tg}scan{th2}")
        if th2 == 0:
            P.op("V", lambda e, carry=carry, lam=lam:
                 e.tensor_copy(carry, lam[:, 1023:1024]),
                 reads=[lam[:, 1023:1024]], writes=[carry], tag=f"{tg}carry")
        sseq = KVC[0:64, 2048:3072]
        P.op("V", lambda e, sseq=sseq, lam=lam: e.tensor_copy(sseq, lam),
             reads=[lam], writes=[sseq], tag=f"{tg}scast{th2}")
        wc = STRM[:, 512:1024]
        P.dma(wc[0:64, 0:512], W["L0_attn_b_C_mat"].ap()[:, :], queue="G2",
              tag=f"{tg}wc{th2}")
        for pc in range(4):
            for tl in range(2):
                ps = PSF[(pc + tl) % 2]
                P.op("T", lambda e, ps=ps, wc=wc, pc=pc, sseq=sseq, tl=tl:
                     e.matmul(ps[:, 0:512], wc[0:64, pc * 128:(pc + 1) * 128],
                              sseq[:, tl * 512:(tl + 1) * 512],
                              start=True, stop=True),
                     reads=[wc[0:64, pc * 128:(pc + 1) * 128],
                            sseq[:, tl * 512:(tl + 1) * 512]],
                     writes=[ps[:, 0:512]], tag=f"{tg}ttm{th2}_{pc}_{tl}")
                t_sl = BIGB[:, 4096 + pc * 1024 + tl * 512:4096 + pc * 1024 + (tl + 1) * 512]
                tt = BIGB[:, 8192 + pc * 1024 + tl * 512:8192 + pc * 1024 + (tl + 1) * 512]
                P.op("V", lambda e, tt=tt, t_sl=t_sl, ps=ps:
                     e.tensor_tensor(tt, t_sl, ps[:, 0:512], ALU.add),
                     reads=[t_sl, ps[:, 0:512]], writes=[tt],
                     tag=f"{tg}tta{th2}_{pc}_{tl}")
        for dmc in range(8):
            wfp = STRM[:, 0:512] if dmc % 2 == 0 else KVC[:, 0:512]
            src = W["L0_attn_b_W_fromP"].ap()[:, dmc * 128:(dmc + 1) * 128]
            src = src.rearrange("(pc p) r -> p pc r", p=128)
            P.dma(wfp.rearrange("p (pc r) -> p pc r", r=128), src, queue="G2",
                  tag=f"{tg}wfp{th2}_{dmc}")
            for tl, tc in enumerate(tcs):
                ps = PSF[2 + (tl % 2)]
                for pc in range(4):
                    tt = BIGB[:, 8192 + pc * 1024 + tl * 512:8192 + pc * 1024 + (tl + 1) * 512]
                    P.op("T", lambda e, ps=ps, wfp=wfp, pc=pc, tt=tt:
                         e.matmul(ps[:, 0:512], wfp[:, pc * 128:(pc + 1) * 128], tt,
                                  start=(pc == 0), stop=(pc == 3)),
                         reads=[wfp[:, pc * 128:(pc + 1) * 128], tt] +
                               ([ps[:, 0:512]] if pc else []),
                         writes=[ps[:, 0:512]], tag=f"{tg}hm{th2}_{dmc}_{tc}_{pc}")
                ev = F32S[:, 2048:2560]
                P.op("V", lambda e, ev=ev, ps=ps: e.tensor_copy(ev, ps[:, 0:512]),
                     reads=[ps[:, 0:512]], writes=[ev], tag=f"{tg}he{th2}_{dmc}_{tl}")
                for j in range(4):
                    pt = PSF[j % 2]
                    blk = ev[:, j * 128:(j + 1) * 128]
                    P.op("T", lambda e, pt=pt, blk=blk:
                         e.transpose(pt[:, 0:128], blk, c["IDF"][:, :]),
                         reads=[blk, c["IDF"][:, :]], writes=[pt[:, 0:128]],
                         tag=f"{tg}htr{th2}_{dmc}_{tl}_{j}")
                    tcg = tc * 4 + j
                    rsl = RESID[:, tcg * D + dmc * 128:tcg * D + (dmc + 1) * 128]
                    P.op("V", lambda e, rsl=rsl, pt=pt:
                         e.tensor_tensor(rsl, rsl, pt[:, 0:128], ALU.add),
                         reads=[rsl, pt[:, 0:128]], writes=[rsl],
                         tag=f"{tg}hr{th2}_{dmc}_{tl}_{j}")


def sub_l1_mla(c, idx):
    tg = f"s{idx}"
    P, W, aux = c["P"], c["W"], c["aux"]
    NFM, BIGA, BIGB, F32S, BF16S, BIAS, KVC, STRM, STAT = (
        c["NFM"], c["BIGA"], c["BIGB"], c["F32S"], c["BF16S"], c["BIAS"],
        c["KVC"], c["STRM"], c["STAT"])
    RESID, PSF, PSB, IDF, IDB, ONEB = (c["RESID"], c["PSF"], c["PSB"], c["IDF"],
                                       c["IDB"], c["ONEB"])
    layernorm_and_transpose(c, tg, mode="plain")

    def kr_ap(h, c0, cn):
        ch = h // 4
        tt = BF16S if ch < 2 else BIAS
        base = (ch % 2) * 2048
        r0 = (h % 4) * 32
        return tt[r0:r0 + 32, base + c0:base + c0 + cn]

    gkv = STAT[:, 33:35]
    for cc in range(2):
        P.dma(gkv[:, cc:cc + 1],
              W["L1_attn_a_g_kv"].ap()[cc * 128:(cc + 1) * 128].unsqueeze(1),
              queue="Y", tag=f"{tg}gkv{cc}")
    for cc in range(2):
        wkv = STRM[:, 0:1024]
        src = W["L1_attn_a_W_kv"].ap()[:, cc * 128:(cc + 1) * 128]
        src = src.rearrange("(dc p) r -> p dc r", p=128)
        P.dma(wkv.rearrange("p (dc r) -> p dc r", r=128), src, queue="G2",
              tag=f"{tg}wkv{cc}")
        for tc in range(4):
            ps = PSF[tc % 2]
            for dc in range(8):
                P.op("T", lambda e, ps=ps, wkv=wkv, dc=dc, tc=tc:
                     e.matmul(ps[:, 0:512], wkv[:, dc * 128:(dc + 1) * 128],
                              NFM[:, dc * T + tc * 512:dc * T + (tc + 1) * 512],
                              start=(dc == 0), stop=(dc == 7)),
                     reads=[wkv[:, dc * 128:(dc + 1) * 128],
                            NFM[:, dc * T + tc * 512:dc * T + (tc + 1) * 512]] +
                           ([ps[:, 0:512]] if dc else []),
                     writes=[ps[:, 0:512]], tag=f"{tg}kvm{cc}_{tc}_{dc}")
            dst = KVC[:, cc * 2048 + tc * 512:cc * 2048 + (tc + 1) * 512]
            P.op("S", lambda e, dst=dst, ps=ps: e.copy(dst, ps[:, 0:512]),
                 reads=[ps[:, 0:512]], writes=[dst], tag=f"{tg}kve{cc}_{tc}")
    eps6 = c["const_col"](1e-6)
    for tc in range(4):
        pss = PSF[2]
        for cc in range(2):
            sl = KVC[:, cc * 2048 + tc * 512:cc * 2048 + (tc + 1) * 512]
            sq = BF16S[:, 0:512]
            P.op("S", lambda e, sq=sq, sl=sl: e.activation(sq, sl, AF.Square),
                 reads=[sl], writes=[sq], tag=f"{tg}sq{tc}_{cc}")
            P.op("T", lambda e, pss=pss, sq=sq, cc=cc:
                 e.matmul(pss[0:1, 0:512], ONEB[:, 0:1], sq,
                          start=(cc == 0), stop=(cc == 1)),
                 reads=[ONEB[:, 0:1], sq] + ([pss[0:1, 0:512]] if cc else []),
                 writes=[pss[0:1, 0:512]], tag=f"{tg}ssm{tc}_{cc}")
        rr = F32S[0:1, 2048:2560]
        P.op("S", lambda e, rr=rr, pss=pss, eps6=eps6:
             e.activation(rr, pss[0:1, 0:512], AF.Sqrt, bias=eps6[0:1, 0:1],
                          scale=1.0 / 256.0),
             reads=[pss[0:1, 0:512], eps6], writes=[rr], tag=f"{tg}sd{tc}")
        P.op("V", lambda e, rr=rr: e.reciprocal(rr, rr), reads=[rr], writes=[rr],
             tag=f"{tg}sr{tc}")
        rrb = BF16S[0:1, 512:1024]
        P.op("V", lambda e, rrb=rrb, rr=rr: e.tensor_copy(rrb, rr),
             reads=[rr], writes=[rrb], tag=f"{tg}src{tc}")
        psb = PSF[3]
        P.op("T", lambda e, psb=psb, rrb=rrb:
             e.matmul(psb[:, 0:512], ONEB[0:1, 0:128], rrb, start=True, stop=True),
             reads=[ONEB[0:1, 0:128], rrb], writes=[psb[:, 0:512]],
             tag=f"{tg}sb{tc}")
        for cc in range(2):
            sl = KVC[:, cc * 2048 + tc * 512:cc * 2048 + (tc + 1) * 512]
            P.op("V", lambda e, sl=sl, gkv=gkv, cc=cc, psb=psb:
                 e.scalar_tensor_tensor(sl, sl, gkv[:, cc:cc + 1], psb[:, 0:512],
                                        ALU.mult, ALU.mult),
                 reads=[sl, gkv, psb[:, 0:512]], writes=[sl],
                 tag=f"{tg}kvn{tc}_{cc}")
    for jc in range(8):
        wkd = STRM[:, (jc % 2) * 256:(jc % 2) * 256 + 256]
        src = W["L1_attn_a_W_kdec"].ap()[:, jc * 128:(jc + 1) * 128]
        src = src.rearrange("(cc p) r -> p cc r", p=128)
        P.dma(wkd.rearrange("p (cc r) -> p cc r", r=128), src, queue="G2",
              tag=f"{tg}wkd{jc}")
        for tc in range(4):
            ps = PSF[tc % 2]
            for cc in range(2):
                sl = KVC[:, cc * 2048 + tc * 512:cc * 2048 + (tc + 1) * 512]
                P.op("T", lambda e, ps=ps, wkd=wkd, cc=cc, sl=sl:
                     e.matmul(ps[:, 0:512], wkd[:, cc * 128:(cc + 1) * 128], sl,
                              start=(cc == 0), stop=(cc == 1)),
                     reads=[wkd[:, cc * 128:(cc + 1) * 128], sl] +
                           ([ps[:, 0:512]] if cc else []),
                     writes=[ps[:, 0:512]], tag=f"{tg}kdm{jc}_{tc}_{cc}")
            dst = BIGA[:, jc * 2048 + tc * 512:jc * 2048 + (tc + 1) * 512]
            P.op("S", lambda e, dst=dst, ps=ps: e.copy(dst, ps[:, 0:512]),
                 reads=[ps[:, 0:512]], writes=[dst], tag=f"{tg}kde{jc}_{tc}")
    wv0 = STRM[:, 0:1024]
    src = W["L1_attn_a_W_vdec"].ap()[0:128, :]
    P.dma(wv0, src, queue="G2", tag=f"{tg}wv0")
    wv1 = F32S[:, 0:1024]
    wv1 = BIAS[:, 0:1024]
    P.dma(wv1, W["L1_attn_a_W_vdec"].ap()[128:256, :], queue="G2", tag=f"{tg}wv1")
    for tc in range(NTC):
        for vh in range(2):
            ps = PSF[vh]
            for cc in range(2):
                wv = wv0 if cc == 0 else wv1
                lh = KVC[:, cc * 2048 + tc * 128:cc * 2048 + (tc + 1) * 128]
                P.op("T", lambda e, ps=ps, lh=lh, wv=wv, vh=vh, cc=cc:
                     e.matmul(ps[:, 0:512], lh, wv[:, vh * 512:(vh + 1) * 512],
                              start=(cc == 0), stop=(cc == 1)),
                     reads=[lh, wv[:, vh * 512:(vh + 1) * 512]] +
                           ([ps[:, 0:512]] if cc else []),
                     writes=[ps[:, 0:512]], tag=f"{tg}vm{tc}_{vh}_{cc}")
            dst = BIGB[:, tc * 1024 + vh * 512:tc * 1024 + (vh + 1) * 512]
            P.op("S", lambda e, dst=dst, ps=ps: e.copy(dst, ps[:, 0:512]),
                 reads=[ps[:, 0:512]], writes=[dst], tag=f"{tg}ve{tc}_{vh}")
    for ch in range(4):
        wkr = STRM[:, 0:1024]
        src = W["L1_attn_a_W_krope"].ap()[:, ch * 128:(ch + 1) * 128]
        src = src.rearrange("(dc p) r -> p dc r", p=128)
        P.dma(wkr.rearrange("p (dc r) -> p dc r", r=128), src, queue="G2",
              tag=f"{tg}wkr{ch}")
        dtt = BF16S if ch < 2 else BIAS
        base = (ch % 2) * 2048
        for tc in range(4):
            ps = PSF[tc % 2]
            for dc in range(8):
                P.op("T", lambda e, ps=ps, wkr=wkr, dc=dc, tc=tc:
                     e.matmul(ps[:, 0:512], wkr[:, dc * 128:(dc + 1) * 128],
                              NFM[:, dc * T + tc * 512:dc * T + (tc + 1) * 512],
                              start=(dc == 0), stop=(dc == 7)),
                     reads=[wkr[:, dc * 128:(dc + 1) * 128],
                            NFM[:, dc * T + tc * 512:dc * T + (tc + 1) * 512]] +
                           ([ps[:, 0:512]] if dc else []),
                     writes=[ps[:, 0:512]], tag=f"{tg}krm{ch}_{tc}_{dc}")
            raw = STRM[:, 0:512]
            raw = BF16S[:, 512:1024] if False else None
            rawf = F32S[:, 0:512]
            P.op("V", lambda e, rawf=rawf, ps=ps: e.tensor_copy(rawf, ps[:, 0:512]),
                 reads=[ps[:, 0:512]], writes=[rawf], tag=f"{tg}krr{ch}_{tc}")
            rot = F32S[:, 512:1024]
            for g in range(4):
                P.dma(rot[g * 32:g * 32 + 16, :],
                      rawf[g * 32 + 16:g * 32 + 32, :], queue="G2",
                      reads=[rawf], writes=[rot], tag=f"{tg}rot1{ch}_{tc}_{g}")
                P.dma(rot[g * 32 + 16:g * 32 + 32, :],
                      rawf[g * 32:g * 32 + 16, :], queue="G2",
                      reads=[rawf], writes=[rot], tag=f"{tg}rot2{ch}_{tc}_{g}")
            cosl = F32S[:, 1024:1536]
            sinl = F32S[:, 1536:2048]
            P.dma(cosl, aux["aux_cos_k"].ap()[ch * 128:(ch + 1) * 128,
                                              tc * 512:(tc + 1) * 512],
                  queue="G2", tag=f"{tg}cos{ch}_{tc}")
            P.dma(sinl, aux["aux_sin_k"].ap()[ch * 128:(ch + 1) * 128,
                                              tc * 512:(tc + 1) * 512],
                  queue="G2", tag=f"{tg}sin{ch}_{tc}")
            P.op("V", lambda e, rawf=rawf, cosl=cosl:
                 e.tensor_tensor(rawf, rawf, cosl, ALU.mult),
                 reads=[rawf, cosl], writes=[rawf], tag=f"{tg}kc{ch}_{tc}")
            P.op("V", lambda e, rot=rot, sinl=sinl:
                 e.tensor_tensor(rot, rot, sinl, ALU.mult),
                 reads=[rot, sinl], writes=[rot], tag=f"{tg}ks{ch}_{tc}")
            dst = dtt[:, base + tc * 512:base + (tc + 1) * 512]
            P.op("V", lambda e, dst=dst, rawf=rawf, rot=rot:
                 e.tensor_tensor(dst, rawf, rot, ALU.add),
                 reads=[rawf, rot], writes=[dst], tag=f"{tg}ka{ch}_{tc}")
    for th in range(2):
        for qcc in range(4):
            wq = STRM[:, 0:1024]
            src = W["L1_attn_a_W_q"].ap()[:, qcc * 128:(qcc + 1) * 128]
            src = src.rearrange("(dc p) r -> p dc r", p=128)
            P.dma(wq.rearrange("p (dc r) -> p dc r", r=128), src, queue="G2",
                  tag=f"{tg}wq{th}_{qcc}")
            for tl2 in range(2):
                tc = th * 2 + tl2
                ps = PSF[tl2 % 2]
                for dc in range(8):
                    P.op("T", lambda e, ps=ps, wq=wq, dc=dc, tc=tc:
                         e.matmul(ps[:, 0:512], wq[:, dc * 128:(dc + 1) * 128],
                                  NFM[:, dc * T + tc * 512:dc * T + (tc + 1) * 512],
                                  start=(dc == 0), stop=(dc == 7)),
                         reads=[wq[:, dc * 128:(dc + 1) * 128],
                                NFM[:, dc * T + tc * 512:dc * T + (tc + 1) * 512]] +
                               ([ps[:, 0:512]] if dc else []),
                         writes=[ps[:, 0:512]], tag=f"{tg}qm{th}_{qcc}_{tc}_{dc}")
                dst = KVC[:, qcc * 1024 + tl2 * 512:qcc * 1024 + (tl2 + 1) * 512]
                P.op("S", lambda e, dst=dst, ps=ps: e.copy(dst, ps[:, 0:512]),
                     reads=[ps[:, 0:512]], writes=[dst], tag=f"{tg}qe{th}_{qcc}_{tl2}")
        for jc in range(8):
            wqd = STRM[:, (jc % 2) * 512:(jc % 2) * 512 + 512]
            src = W["L1_attn_a_W_qdec"].ap()[:, jc * 128:(jc + 1) * 128]
            src = src.rearrange("(qc p) r -> p qc r", p=128)
            P.dma(wqd.rearrange("p (qc r) -> p qc r", r=128), src, queue="G2",
                  tag=f"{tg}wqd{th}_{jc}")
            for tl2 in range(2):
                ps = PSF[2 + tl2 % 2]
                for qc in range(4):
                    lh = wqd[:, qc * 128:(qc + 1) * 128]
                    rh = KVC[:, qc * 1024 + tl2 * 512:qc * 1024 + (tl2 + 1) * 512]
                    P.op("T", lambda e, ps=ps, lh=lh, rh=rh, qc=qc:
                         e.matmul(ps[:, 0:512], lh, rh,
                                  start=(qc == 0), stop=(qc == 3)),
                         reads=[lh, rh] + ([ps[:, 0:512]] if qc else []),
                         writes=[ps[:, 0:512]], tag=f"{tg}qdm{th}_{jc}_{tl2}_{qc}")
                dst = NFM[:, jc * T + th * 1024 + tl2 * 512:
                          jc * T + th * 1024 + (tl2 + 1) * 512]
                P.op("S", lambda e, dst=dst, ps=ps:
                     e.activation(dst, ps[:, 0:512], AF.Copy,
                                  scale=float(1.0 / np.sqrt(96.0))),
                     reads=[ps[:, 0:512]], writes=[dst], tag=f"{tg}qde{th}_{jc}_{tl2}")
        QR = KVC
        for ch in range(4):
            wqr = STRM[:, 0:512]
            src = W["L1_attn_a_W_qrope"].ap()[:, ch * 128:(ch + 1) * 128]
            src = src.rearrange("(qc p) r -> p qc r", p=128)
            P.dma(wqr.rearrange("p (qc r) -> p qc r", r=128), src, queue="G2",
                  tag=f"{tg}wqr{th}_{ch}")
            for tl2 in range(2):
                ps = PSF[4]
                for qc in range(4):
                    lh = wqr[:, qc * 128:(qc + 1) * 128]
                    rh = KVC[:, qc * 1024 + tl2 * 512:qc * 1024 + (tl2 + 1) * 512]
                    P.op("T", lambda e, ps=ps, lh=lh, rh=rh, qc=qc:
                         e.matmul(ps[:, 0:512], lh, rh,
                                  start=(qc == 0), stop=(qc == 3)),
                         reads=[lh, rh] + ([ps[:, 0:512]] if qc else []),
                         writes=[ps[:, 0:512]], tag=f"{tg}qrm{th}_{ch}_{tl2}_{qc}")
                rawf = F32S[:, 0:512]
                P.op("V", lambda e, rawf=rawf, ps=ps: e.tensor_copy(rawf, ps[:, 0:512]),
                     reads=[ps[:, 0:512]], writes=[rawf], tag=f"{tg}qrr{th}_{ch}_{tl2}")
                rot = F32S[:, 512:1024]
                for g in range(4):
                    P.dma(rot[g * 32:g * 32 + 16, :],
                          rawf[g * 32 + 16:g * 32 + 32, :], queue="G2",
                          reads=[rawf], writes=[rot],
                          tag=f"{tg}qrot1{th}_{ch}_{tl2}_{g}")
                    P.dma(rot[g * 32 + 16:g * 32 + 32, :],
                          rawf[g * 32:g * 32 + 16, :], queue="G2",
                          reads=[rawf], writes=[rot],
                          tag=f"{tg}qrot2{th}_{ch}_{tl2}_{g}")
                cosl = F32S[:, 1024:1536]
                sinl = F32S[:, 1536:2048]
                t0 = th * 1024 + tl2 * 512
                P.dma(cosl, aux["aux_cos_q"].ap()[ch * 128:(ch + 1) * 128,
                                                  t0:t0 + 512],
                      queue="G2", tag=f"{tg}qcos{th}_{ch}_{tl2}")
                P.dma(sinl, aux["aux_sin_q"].ap()[ch * 128:(ch + 1) * 128,
                                                  t0:t0 + 512],
                      queue="G2", tag=f"{tg}qsin{th}_{ch}_{tl2}")
                P.op("V", lambda e, rawf=rawf, cosl=cosl:
                     e.tensor_tensor(rawf, rawf, cosl, ALU.mult),
                     reads=[rawf, cosl], writes=[rawf], tag=f"{tg}qc2{th}_{ch}_{tl2}")
                P.op("V", lambda e, rot=rot, sinl=sinl:
                     e.tensor_tensor(rot, rot, sinl, ALU.mult),
                     reads=[rot, sinl], writes=[rot], tag=f"{tg}qs2{th}_{ch}_{tl2}")
                dst = QR[:, ch * 1024 + tl2 * 512:ch * 1024 + (tl2 + 1) * 512]
                P.op("V", lambda e, dst=dst, rawf=rawf, rot=rot:
                     e.tensor_tensor(dst, rawf, rot, ALU.add),
                     reads=[rawf, rot], writes=[dst], tag=f"{tg}qa2{th}_{ch}_{tl2}")
        for ql in range(8):
            qt = th * 8 + ql
            msk = STRM[:, 0:512]
            P.dma(msk, aux["aux_cmask"].ap()[:, (qt % 4) * 512:(qt % 4 + 1) * 512],
                  queue="G2", tag=f"{tg}msk{qt}")
            nkt = qt // 4 + 1
            Of = F32S[:, 0:1024]
            for h in range(16):
                acc = F32S[:, 1024 + 64 * (h % 2):1088 + 64 * (h % 2)]
                mcol = STAT[:, 36 + (h % 2) * 8:37 + (h % 2) * 8]
                lcol = STAT[:, 37 + (h % 2) * 8:38 + (h % 2) * 8]
                m2 = STAT[:, 38 + (h % 2) * 8:39 + (h % 2) * 8]
                nm2 = STAT[:, 39 + (h % 2) * 8:40 + (h % 2) * 8]
                alp = STAT[:, 40 + (h % 2) * 8:41 + (h % 2) * 8]
                rs = STAT[:, 41 + (h % 2) * 8:42 + (h % 2) * 8]
                P.op("V", lambda e, mcol=mcol: e.memset(mcol, -1e9),
                     writes=[mcol], tag=f"{tg}fm0{qt}_{h}")
                P.op("V", lambda e, lcol=lcol: e.memset(lcol, 0.0),
                     writes=[lcol], tag=f"{tg}fl0{qt}_{h}")
                P.op("V", lambda e, acc=acc: e.memset(acc, 0.0),
                     writes=[acc], tag=f"{tg}fa0{qt}_{h}")
                jc, rbase = h // 2, (h % 2) * 64
                rch, rrow = h // 4, (h % 4) * 32
                for kt in range(nkt):
                    ps = PSF[kt % 2]
                    qcsl = NFM[rbase:rbase + 64,
                               jc * T + th * 1024 + ql * 128:
                               jc * T + th * 1024 + (ql + 1) * 128]
                    kcsl = BIGA[rbase:rbase + 64,
                                jc * T + kt * 512:jc * T + (kt + 1) * 512]
                    P.op("T", lambda e, ps=ps, qcsl=qcsl, kcsl=kcsl, rbase=rbase:
                         e.matmul(ps[:, 0:512], qcsl, kcsl, start=True, stop=False,
                                  tile_position=(rbase, 0)),
                         reads=[qcsl, kcsl], writes=[ps[:, 0:512]],
                         tag=f"{tg}sc1_{qt}_{h}_{kt}")
                    qrsl = QR[rrow:rrow + 32,
                              rch * 1024 + ql * 128:rch * 1024 + (ql + 1) * 128]
                    krsl = kr_ap(h, kt * 512, 512)
                    P.op("T", lambda e, ps=ps, qrsl=qrsl, krsl=krsl, rrow=rrow:
                         e.matmul(ps[:, 0:512], qrsl, krsl, start=False, stop=True,
                                  tile_position=(rrow, 0)),
                         reads=[qrsl, krsl, ps[:, 0:512]], writes=[ps[:, 0:512]],
                         tag=f"{tg}sc2_{qt}_{h}_{kt}")
                    if kt == nkt - 1:
                        P.op("V", lambda e, ps=ps, msk=msk:
                             e.tensor_tensor(ps[:, 0:512], ps[:, 0:512], msk,
                                             ALU.add),
                             reads=[ps[:, 0:512], msk], writes=[ps[:, 0:512]],
                             tag=f"{tg}mskadd{qt}_{h}")
                    P.op("V", lambda e, m2=m2, ps=ps:
                         e.reduce_max(m2, ps[:, 0:512], axis=AXX),
                         reads=[ps[:, 0:512]], writes=[m2], tag=f"{tg}tm{qt}_{h}_{kt}")
                    P.op("V", lambda e, m2=m2, mcol=mcol:
                         e.tensor_tensor(m2, m2, mcol, ALU.max),
                         reads=[m2, mcol], writes=[m2], tag=f"{tg}mx{qt}_{h}_{kt}")
                    P.op("V", lambda e, nm2=nm2, m2=m2:
                         e.tensor_scalar_mul(nm2, m2, -1.0),
                         reads=[m2], writes=[nm2], tag=f"{tg}nm{qt}_{h}_{kt}")
                    P.op("S", lambda e, alp=alp, mcol=mcol, nm2=nm2:
                         e.activation(alp, mcol, AF.Exp, bias=nm2),
                         reads=[mcol, nm2], writes=[alp], tag=f"{tg}al{qt}_{h}_{kt}")
                    P.op("V", lambda e, mcol=mcol, m2=m2: e.tensor_copy(mcol, m2),
                         reads=[m2], writes=[mcol], tag=f"{tg}mu{qt}_{h}_{kt}")
                    pbf = STRM[:, 512:1024]
                    P.op("S", lambda e, pbf=pbf, ps=ps, nm2=nm2, rs=rs:
                         e.activation(pbf, ps[:, 0:512], AF.Exp, bias=nm2,
                                      accum_out=rs),
                         reads=[ps[:, 0:512], nm2], writes=[pbf, rs],
                         tag=f"{tg}px{qt}_{h}_{kt}")
                    P.op("V", lambda e, lcol=lcol, alp=alp, rs=rs:
                         e.scalar_tensor_tensor(lcol, lcol, alp, rs,
                                                ALU.mult, ALU.add),
                         reads=[lcol, alp, rs], writes=[lcol],
                         tag=f"{tg}lu{qt}_{h}_{kt}")
                    P.op("S", lambda e, acc=acc, alp=alp:
                         e.activation(acc, acc, AF.Copy, scale=alp),
                         reads=[acc, alp], writes=[acc], tag=f"{tg}as{qt}_{h}_{kt}")
                    pb = PSB[kt % 2]
                    for j in range(4):
                        blk = pbf[:, j * 128:(j + 1) * 128]
                        P.op("T", lambda e, pb=pb, j=j, blk=blk:
                             e.transpose(pb[:, j * 128:(j + 1) * 128], blk,
                                         IDB[:, :]),
                             reads=[blk, IDB[:, :]],
                             writes=[pb[:, j * 128:(j + 1) * 128]],
                             tag=f"{tg}pt{qt}_{h}_{kt}_{j}")
                    P.op("V", lambda e, pbf=pbf, pb=pb:
                         e.tensor_copy(pbf, pb[:, 0:512]),
                         reads=[pb[:, 0:512]], writes=[pbf], tag=f"{tg}pte{qt}_{h}_{kt}")
                    pv = PSF[2 + kt % 2]
                    for j in range(4):
                        vsl = BIGB[:, (kt * 4 + j) * 1024 + h * 64:
                                   (kt * 4 + j) * 1024 + (h + 1) * 64]
                        P.op("T", lambda e, pv=pv, pbf=pbf, j=j, vsl=vsl:
                             e.matmul(pv[:, 0:64], pbf[:, j * 128:(j + 1) * 128],
                                      vsl, start=(j == 0), stop=(j == 3)),
                             reads=[pbf[:, j * 128:(j + 1) * 128], vsl] +
                                   ([pv[:, 0:64]] if j else []),
                             writes=[pv[:, 0:64]], tag=f"{tg}pv{qt}_{h}_{kt}_{j}")
                    P.op("V", lambda e, acc=acc, pv=pv:
                         e.tensor_tensor(acc, acc, pv[:, 0:64], ALU.add),
                         reads=[acc, pv[:, 0:64]], writes=[acc],
                         tag=f"{tg}av{qt}_{h}_{kt}")
                P.op("V", lambda e, lcol=lcol: e.reciprocal(lcol, lcol),
                     reads=[lcol], writes=[lcol], tag=f"{tg}li{qt}_{h}")
                osl = Of[:, h * 64:(h + 1) * 64]
                P.op("S", lambda e, osl=osl, acc=acc, lcol=lcol:
                     e.activation(osl, acc, AF.Copy, scale=lcol),
                     reads=[acc, lcol], writes=[osl], tag=f"{tg}on{qt}_{h}")
            psA, psB2 = PSF[4], PSF[5]
            for jc2 in range(8):
                pb = PSF[jc2 % 2]
                blk = Of[:, jc2 * 128:(jc2 + 1) * 128]
                P.op("T", lambda e, pb=pb, blk=blk:
                     e.transpose(pb[:, 0:128], blk, IDF[:, :]),
                     reads=[blk, IDF[:, :]], writes=[pb[:, 0:128]],
                     tag=f"{tg}ot{qt}_{jc2}")
                otb = STRM[:, 512:640]
                P.op("V", lambda e, otb=otb, pb=pb: e.tensor_copy(otb, pb[:, 0:128]),
                     reads=[pb[:, 0:128]], writes=[otb], tag=f"{tg}ote{qt}_{jc2}")
                wo = STRM[:, 0:512] if False else None
                wot = BF16S[0:128, 0:1024] if False else None
                wobuf = STRM[:, 0:512]
                wobuf = KVC[:, 0:1024] if False else None
                wob = STRM[:, 0:512]
                srcw = W["L1_attn_a_W_o"].ap()[jc2 * 128:(jc2 + 1) * 128, 0:512]
                P.dma(wob, srcw, queue="G2", tag=f"{tg}wo1{qt}_{jc2}")
                P.op("T", lambda e, psA=psA, otb=otb, wob=wob, jc2=jc2:
                     e.matmul(psA[:, 0:512], otb, wob,
                              start=(jc2 == 0), stop=(jc2 == 7)),
                     reads=[otb, wob] + ([psA[:, 0:512]] if jc2 else []),
                     writes=[psA[:, 0:512]], tag=f"{tg}oma{qt}_{jc2}")
                wob2 = STRM[:, 640:1152] if False else None
                wob2 = pbf_dummy = None
                wob2 = STRM[:, 0:512]
                wob2 = msk
                srcw2 = W["L1_attn_a_W_o"].ap()[jc2 * 128:(jc2 + 1) * 128, 512:1024]
                P.dma(wob2, srcw2, queue="G2", tag=f"{tg}wo2{qt}_{jc2}")
                P.op("T", lambda e, psB2=psB2, otb=otb, wob2=wob2, jc2=jc2:
                     e.matmul(psB2[:, 0:512], otb, wob2,
                              start=(jc2 == 0), stop=(jc2 == 7)),
                     reads=[otb, wob2] + ([psB2[:, 0:512]] if jc2 else []),
                     writes=[psB2[:, 0:512]], tag=f"{tg}omb{qt}_{jc2}")
            for dmh, psX in ((0, psA), (1, psB2)):
                rsl = RESID[:, qt * D + dmh * 512:qt * D + (dmh + 1) * 512]
                P.op("V", lambda e, rsl=rsl, psX=psX:
                     e.tensor_tensor(rsl, rsl, psX[:, 0:512], ALU.add),
                     reads=[rsl, psX[:, 0:512]], writes=[rsl],
                     tag=f"{tg}or{qt}_{dmh}")


def sub_l1_ffn(c, idx):
    tg = f"s{idx}"
    P, W, scr = c["P"], c["W"], c["scr"]
    NFM, BIGA, BIGB, F32S, BF16S, BIAS, KVC, STRM, STAT = (
        c["NFM"], c["BIGA"], c["BIGB"], c["F32S"], c["BF16S"], c["BIAS"],
        c["KVC"], c["STRM"], c["STAT"])
    RESID, PSF, PSB, IDB = c["RESID"], c["PSF"], c["PSB"], c["IDB"]
    layernorm_and_transpose(c, tg, mode="int8", wscol=c["ws1_col"])
    for half in range(2):
        tcs = [half * 8 + i for i in range(8)]
        b1b = BIAS[:, 0:4096]
        P.dma(b1b, W["L1_ffn_b1"].ap().unsqueeze(0).partition_broadcast(128),
              queue="G2", tag=f"{tg}b1b{half}")
        for tcl in range(8):
            P.op("V", lambda e, tcl=tcl: e.memset(STAT[:, 32 + tcl:33 + tcl], 0.0),
                 writes=[STAT[:, 32 + tcl:33 + tcl]], tag=f"{tg}rm0{half}_{tcl}")
        for ns in range(8):
            w1 = KVC[:, 0:4096]
            src = scr["s_w1tT"].ap()[:, ns * 512:(ns + 1) * 512]
            src = src.rearrange("(dc p) n -> p dc n", p=128)
            P.dma(w1.rearrange("p (dc n) -> p dc n", n=512), src, queue="Y",
                  tag=f"{tg}w1{half}_{ns}")
            for tcl, tc in enumerate(tcs):
                ps = PSF[tcl % 2]
                for dc in range(8):
                    lh = NFM[:, dc * T + tc * 128:dc * T + (tc + 1) * 128]
                    rh = w1[:, dc * 512:(dc + 1) * 512]
                    P.op("T", lambda e, ps=ps, lh=lh, rh=rh, dc=dc:
                         e.matmul(ps[:, 0:512], lh, rh,
                                  start=(dc == 0), stop=(dc == 7)),
                         reads=[lh, rh] + ([ps[:, 0:512]] if dc else []),
                         writes=[ps[:, 0:512]], tag=f"{tg}h1m{half}_{ns}_{tcl}_{dc}")
                dq = STAT[:, 16 + tc:17 + tc]
                t1 = F32S[:, 0:512]
                P.op("V", lambda e, t1=t1, ps=ps, dq=dq, ns=ns:
                     e.scalar_tensor_tensor(t1, ps[:, 0:512], dq,
                                            b1b[:, ns * 512:(ns + 1) * 512],
                                            ALU.mult, ALU.add),
                     reads=[ps[:, 0:512], dq, b1b[:, ns * 512:(ns + 1) * 512]],
                     writes=[t1], tag=f"{tg}h1d{half}_{ns}_{tcl}")
                hbuf = BIGA if tcl < 4 else BIGB
                hs = hbuf[:, (tcl % 4) * 4096 + ns * 512:(tcl % 4) * 4096 + (ns + 1) * 512]
                P.op("S", lambda e, hs=hs, t1=t1: e.activation(hs, t1, AF.Gelu),
                     reads=[t1], writes=[hs], tag=f"{tg}h1g{half}_{ns}_{tcl}")
                mx = STAT[:, 56 + tcl % 2:57 + tcl % 2]
                P.op("V", lambda e, mx=mx, hs=hs:
                     e.reduce_max(mx, hs, axis=AXX, apply_absolute_value=True),
                     reads=[hs], writes=[mx], tag=f"{tg}h1x{half}_{ns}_{tcl}")
                rm = STAT[:, 32 + tcl:33 + tcl]
                P.op("V", lambda e, rm=rm, mx=mx:
                     e.tensor_tensor(rm, rm, mx, ALU.max),
                     reads=[rm, mx], writes=[rm], tag=f"{tg}h1u{half}_{ns}_{tcl}")
        b2b = BIAS[:, 0:1024]
        P.dma(b2b, W["L1_ffn_b2"].ap().unsqueeze(0).partition_broadcast(128),
              queue="G2", tag=f"{tg}b2b{half}")
        for tcl in range(8):
            rm = STAT[:, 32 + tcl:33 + tcl]
            scl2 = STAT[:, 40 + tcl:41 + tcl]
            dq2 = STAT[:, 48 + tcl:49 + tcl]
            P.op("V", lambda e, rm=rm: e.tensor_scalar_max(rm, rm, 1e-5),
                 reads=[rm], writes=[rm], tag=f"{tg}q2c{half}_{tcl}")
            P.op("V", lambda e, scl2=scl2, rm=rm: e.reciprocal(scl2, rm),
                 reads=[rm], writes=[scl2], tag=f"{tg}q2r{half}_{tcl}")
            P.op("V", lambda e, scl2=scl2: e.tensor_scalar_mul(scl2, scl2, 127.0),
                 reads=[scl2], writes=[scl2], tag=f"{tg}q2s{half}_{tcl}")
            P.op("V", lambda e, dq2=dq2, rm=rm, w=c["ws2_col"]:
                 e.tensor_tensor(dq2, rm, w, ALU.mult),
                 reads=[rm, c["ws2_col"]], writes=[dq2], tag=f"{tg}q2d{half}_{tcl}")
            P.op("V", lambda e, dq2=dq2: e.tensor_scalar_mul(dq2, dq2, 1.0 / 127.0),
                 reads=[dq2], writes=[dq2], tag=f"{tg}q2e{half}_{tcl}")
        for ns in range(8):
            w2 = KVC[:, 0:4096]
            src = scr["s_w2tT"].ap()[ns * 512:(ns + 1) * 512, :]
            src = src.rearrange("(j p) m -> p j m", p=128)
            P.dma(w2.rearrange("p (j m) -> p j m", m=1024), src, queue="Y",
                  tag=f"{tg}w2{half}_{ns}")
            for tcl, tc in enumerate(tcs):
                hbuf = BIGA if tcl < 4 else BIGB
                hs = hbuf[:, (tcl % 4) * 4096 + ns * 512:(tcl % 4) * 4096 + (ns + 1) * 512]
                scl2 = STAT[:, 40 + tcl:41 + tcl]
                dq2 = STAT[:, 48 + tcl:49 + tcl]
                v = F32S[:, 0:512]
                m = F32S[:, 512:1024]
                P.op("V", lambda e, v=v, hs=hs, scl2=scl2:
                     e.tensor_scalar_mul(v, hs, scl2),
                     reads=[hs, scl2], writes=[v], tag=f"{tg}qv{half}_{ns}_{tcl}")
                P.op("V", lambda e, m=m, v=v:
                     e.tensor_single_scalar(m, v, 12582912.0, ALU.add),
                     reads=[v], writes=[m], tag=f"{tg}qm{half}_{ns}_{tcl}")
                t1 = F32S[:, 1024:1536]
                P.op("V", lambda e, t1=t1, m=m:
                     e.tensor_single_scalar(t1, m, -12582912.0, ALU.add),
                     reads=[m], writes=[t1], tag=f"{tg}qt{half}_{ns}_{tcl}")
                xqs = BF16S[:, 0:512]
                P.op("V", lambda e, xqs=xqs, t1=t1, dq2=dq2:
                     e.tensor_scalar_mul(xqs, t1, dq2),
                     reads=[t1, dq2], writes=[xqs], tag=f"{tg}qx{half}_{ns}_{tcl}")
                pb = PSB[tcl % 2]
                for j in range(4):
                    blk = xqs[:, j * 128:(j + 1) * 128]
                    P.op("T", lambda e, pb=pb, j=j, blk=blk:
                         e.transpose(pb[:, j * 128:(j + 1) * 128], blk, IDB[:, :]),
                         reads=[blk, IDB[:, :]],
                         writes=[pb[:, j * 128:(j + 1) * 128]],
                         tag=f"{tg}qT{half}_{ns}_{tcl}_{j}")
                xqf = BF16S[:, 512:1024]
                P.op("V", lambda e, xqf=xqf, pb=pb: e.tensor_copy(xqf, pb[:, 0:512]),
                     reads=[pb[:, 0:512]], writes=[xqf], tag=f"{tg}qe2{half}_{ns}_{tcl}")
                for dmh in range(2):
                    psY = PSF[2 + dmh]
                    for j in range(4):
                        lh = xqf[:, j * 128:(j + 1) * 128]
                        rh = w2[:, j * 1024 + dmh * 512:j * 1024 + dmh * 512 + 512]
                        P.op("T", lambda e, psY=psY, lh=lh, rh=rh, j=j:
                             e.matmul(psY[:, 0:512], lh, rh,
                                      start=(j == 0), stop=(j == 3)),
                             reads=[lh, rh] + ([psY[:, 0:512]] if j else []),
                             writes=[psY[:, 0:512]],
                             tag=f"{tg}dm{half}_{ns}_{tcl}_{dmh}_{j}")
                    rsl = RESID[:, tc * D + dmh * 512:tc * D + dmh * 512 + 512]
                    P.op("V", lambda e, rsl=rsl, psY=psY:
                         e.tensor_tensor(rsl, rsl, psY[:, 0:512], ALU.add),
                         reads=[rsl, psY[:, 0:512]], writes=[rsl],
                         tag=f"{tg}ra{half}_{ns}_{tcl}_{dmh}")
        for tcl, tc in enumerate(tcs):
            for dmh in range(2):
                rsl = RESID[:, tc * D + dmh * 512:tc * D + dmh * 512 + 512]
                P.op("V", lambda e, rsl=rsl, dmh=dmh:
                     e.tensor_tensor(rsl, rsl, b2b[:, dmh * 512:(dmh + 1) * 512],
                                     ALU.add),
                     reads=[rsl, b2b[:, dmh * 512:(dmh + 1) * 512]], writes=[rsl],
                     tag=f"{tg}b2a{half}_{tcl}_{dmh}")


def sub_l1_gdn(c, idx):
    tg = f"s{idx}"
    P, W, aux = c["P"], c["W"], c["aux"]
    NFM, BIGA, BIGB, F32S, BF16S, BIAS, KVC, STRM, STAT = (
        c["NFM"], c["BIGA"], c["BIGB"], c["F32S"], c["BF16S"], c["BIAS"],
        c["KVC"], c["STRM"], c["STAT"])
    RESID, PSF, PSB, IDF, IDB, ONEB = (c["RESID"], c["PSF"], c["PSB"], c["IDF"],
                                       c["IDB"], c["ONEB"])
    one_col = c["const_col"](1.0)
    lnw = BIAS[:, 0:1024]
    lnb = BIAS[:, 1024:2048]
    P.dma(lnw, W["L1_attn_b_ln_w"].ap().unsqueeze(0).partition_broadcast(128),
          queue="G2", tag=f"{tg}lnw")
    P.dma(lnb, W["L1_attn_b_ln_b"].ap().unsqueeze(0).partition_broadcast(128),
          queue="G2", tag=f"{tg}lnb")
    layernorm_and_transpose(c, tg, mode="double", lnw=lnw, lnb=lnb)
    mskN = BIAS[:, 2048:2176]
    mskM = BIAS[:, 2176:2304]
    P.dma(mskN, aux["aux_maskNstrict"].ap()[:, :], queue="Y", tag=f"{tg}mn")
    P.dma(mskM, aux["aux_maskMincl"].ap()[:, :], queue="Y", tag=f"{tg}mm")

    wba = STRM[:, 0:64]
    src = W["L1_attn_b_Wb"].ap().rearrange("(dc p) r -> p dc r", p=128)
    P.dma(wba.rearrange("p (dc r) -> p dc r", r=4)[:, 0:8, :], src, queue="G2",
          tag=f"{tg}wb")
    src = W["L1_attn_b_Wa"].ap().rearrange("(dc p) r -> p dc r", p=128)
    P.dma(wba.rearrange("p (dc r) -> p dc r", r=4)[:, 8:16, :], src, queue="G2",
          tag=f"{tg}wa")
    alog = STAT[0:4, 110:111]
    dtb = STAT[0:4, 111:112]
    P.dma(alog, W["L1_attn_b_A_log"].ap().unsqueeze(1), queue="Y", tag=f"{tg}al")
    P.dma(dtb, W["L1_attn_b_dt_bias"].ap().unsqueeze(1), queue="Y", tag=f"{tg}db")
    nega = STAT[0:4, 112:113]
    P.op("S", lambda e: e.activation(nega, alog, AF.Exp), reads=[alog],
         writes=[nega], tag=f"{tg}ne")
    P.op("V", lambda e: e.tensor_scalar_mul(nega, nega, -1.0), reads=[nega],
         writes=[nega], tag=f"{tg}ne2")
    PK = 2560
    lam_hc = F32S[0:64, PK:PK + 128]
    bet_hc = F32S[0:64, PK + 128:PK + 256]
    eL_hc = F32S[0:64, PK + 256:PK + 384]
    eC_hc = F32S[0:64, PK + 384:PK + 512]
    bg = F32S[0:4, 0:2048]
    va = F32S[0:4, 2048:2560]
    for tc in range(4):
        psa2 = PSF[tc % 2]
        for dc in range(8):
            nsl = NFM[:, dc * T + tc * 512:dc * T + (tc + 1) * 512]
            P.op("T", lambda e, psa2=psa2, dc=dc, nsl=nsl:
                 e.matmul(psa2[0:4, 0:512], wba[:, 32 + dc * 4:32 + (dc + 1) * 4],
                          nsl, start=(dc == 0), stop=(dc == 7)),
                 reads=[wba[:, 32 + dc * 4:32 + (dc + 1) * 4], nsl] +
                       ([psa2[0:4, 0:512]] if dc else []),
                 writes=[psa2[0:4, 0:512]], tag=f"{tg}am{tc}_{dc}")
        dstg = bg[:, tc * 512:(tc + 1) * 512]
        P.op("S", lambda e, va=va, psa2=psa2:
             e.activation(va, psa2[0:4, 0:512], AF.Abs, bias=dtb),
             reads=[psa2[0:4, 0:512], dtb], writes=[va], tag=f"{tg}g1{tc}")
        P.op("S", lambda e, va=va: e.activation(va, va, AF.Exp, scale=-1.0),
             reads=[va], writes=[va], tag=f"{tg}g2{tc}")
        P.op("S", lambda e, va=va: e.activation(va, va, AF.Ln, bias=one_col[0:4, :]),
             reads=[va, one_col], writes=[va], tag=f"{tg}g3{tc}")
        P.op("S", lambda e, dstg=dstg, psa2=psa2:
             e.activation(dstg, psa2[0:4, 0:512], AF.Relu, bias=dtb),
             reads=[psa2[0:4, 0:512], dtb], writes=[dstg], tag=f"{tg}g4{tc}")
        P.op("V", lambda e, dstg=dstg, va=va:
             e.tensor_tensor(dstg, dstg, va, ALU.add),
             reads=[dstg, va], writes=[dstg], tag=f"{tg}g5{tc}")
        P.op("V", lambda e, dstg=dstg:
             e.tensor_scalar_mul(dstg, dstg, nega[0:4, :]),
             reads=[dstg, nega], writes=[dstg], tag=f"{tg}g6{tc}")
    src = bass.AP(bg.tensor, bg.offset, [[3072, 4], [128, 16], [1, 128]])
    P.dma(bass.AP(lam_hc.tensor, lam_hc.offset, [[3072, 64], [1, 128]]), src,
          queue="G2", reads=[bg], writes=[lam_hc], tag=f"{tg}ghc")
    for tc in range(4):
        psb2 = PSF[tc % 2]
        for dc in range(8):
            nsl = NFM[:, dc * T + tc * 512:dc * T + (tc + 1) * 512]
            P.op("T", lambda e, psb2=psb2, dc=dc, nsl=nsl:
                 e.matmul(psb2[0:4, 0:512], wba[:, dc * 4:(dc + 1) * 4], nsl,
                          start=(dc == 0), stop=(dc == 7)),
                 reads=[wba[:, dc * 4:(dc + 1) * 4], nsl] +
                       ([psb2[0:4, 0:512]] if dc else []),
                 writes=[psb2[0:4, 0:512]], tag=f"{tg}bm{tc}_{dc}")
        dstb = bg[:, tc * 512:(tc + 1) * 512]
        P.op("S", lambda e, dstb=dstb, psb2=psb2:
             e.activation(dstb, psb2[0:4, 0:512], AF.Sigmoid),
             reads=[psb2[0:4, 0:512]], writes=[dstb], tag=f"{tg}be{tc}")
    src = bass.AP(bg.tensor, bg.offset, [[3072, 4], [128, 16], [1, 128]])
    P.dma(bass.AP(bet_hc.tensor, bet_hc.offset, [[3072, 64], [1, 128]]), src,
          queue="G2", reads=[bg], writes=[bet_hc], tag=f"{tg}bhc")
    P.op("V", lambda e: e.tensor_tensor_scan(lam_hc, lam_hc, lam_hc, 0.0,
                                             ALU.add, ALU.bypass),
         reads=[lam_hc], writes=[lam_hc], tag=f"{tg}cum")
    P.dma(c["scr"]["s_lam"].ap()[:, :], lam_hc, queue="G2", tag=f"{tg}lamscr")
    P.op("S", lambda e: e.activation(eL_hc, lam_hc, AF.Exp),
         reads=[lam_hc], writes=[eL_hc], tag=f"{tg}eL")
    lamC = STAT[0:64, 113:114]
    P.op("V", lambda e: e.tensor_copy(lamC, lam_hc[:, 127:128]),
         reads=[lam_hc[:, 127:128]], writes=[lamC], tag=f"{tg}lC")
    P.op("S", lambda e: e.activation(eC_hc, lam_hc, AF.Exp, scale=-1.0, bias=lamC),
         reads=[lam_hc, lamC], writes=[eC_hc], tag=f"{tg}eC")

    HK, HV = 256, 512
    for h in range(4):
        def proj_conv(kind, wname, cname, nchunks, rowbase, dst_off, l2=None):
            for cc2 in range(nchunks):
                ch = rowbase + cc2
                wpb = STRM[:, 0:1024]
                src = W[wname].ap()[:, ch * 128:(ch + 1) * 128]
                src = src.rearrange("(dc p) r -> p dc r", p=128)
                P.dma(wpb.rearrange("p (dc r) -> p dc r", r=128), src,
                      queue="G2", tag=f"{tg}{kind}w{h}_{cc2}")
                cw = STAT[:, 80:84]
                cb = None
                P.dma(cw, W[cname].ap()[ch * 128:(ch + 1) * 128, :], queue="Y",
                      tag=f"{tg}{kind}cw{h}_{cc2}")
                x_t = F32S[:, 0:2048]
                for tc in range(4):
                    ps = PSF[tc % 2]
                    for dc in range(8):
                        nsl = NFM[:, dc * T + tc * 512:dc * T + (tc + 1) * 512]
                        P.op("T", lambda e, ps=ps, wpb=wpb, dc=dc, nsl=nsl:
                             e.matmul(ps[:, 0:512], wpb[:, dc * 128:(dc + 1) * 128],
                                      nsl, start=(dc == 0), stop=(dc == 7)),
                             reads=[wpb[:, dc * 128:(dc + 1) * 128], nsl] +
                                   ([ps[:, 0:512]] if dc else []),
                             writes=[ps[:, 0:512]],
                             tag=f"{tg}{kind}m{h}_{cc2}_{tc}_{dc}")
                    dstx = x_t[:, tc * 512:(tc + 1) * 512]
                    P.op("V", lambda e, dstx=dstx, ps=ps:
                         e.tensor_copy(dstx, ps[:, 0:512]),
                         reads=[ps[:, 0:512]], writes=[dstx],
                         tag=f"{tg}{kind}e{h}_{cc2}_{tc}")
                acc = F32S[:, 2048:2560]
                for tc in range(4):
                    a = acc[:, 0:512]
                    xs = x_t[:, tc * 512:(tc + 1) * 512]
                    P.op("V", lambda e, a=a, xs=xs, cw=cw:
                         e.tensor_scalar_mul(a, xs, cw[:, 3:4]),
                         reads=[xs, cw], writes=[a], tag=f"{tg}{kind}c3{h}_{cc2}_{tc}")
                    for kk in range(3):
                        sh = 3 - kk
                        lo = tc * 512 - sh
                        if lo < 0:
                            aa = a[:, sh:512]
                            xx = x_t[:, 0:512 - sh]
                        else:
                            aa = a[:, 0:512]
                            xx = x_t[:, lo:lo + 512]
                        P.op("V", lambda e, aa=aa, xx=xx, cw=cw, kk=kk:
                             e.scalar_tensor_tensor(aa, xx, cw[:, kk:kk + 1], aa,
                                                    ALU.mult, ALU.add),
                             reads=[xx, cw, aa], writes=[aa],
                             tag=f"{tg}{kind}c{kk}_{h}_{cc2}_{tc}")
                    dstu = BIGA[:, dst_off + cc2 * 2048 + tc * 512:
                                dst_off + cc2 * 2048 + (tc + 1) * 512]
                    P.op("S", lambda e, a=a, dstu=dstu:
                         e.activation(dstu, a, AF.Silu),
                         reads=[a], writes=[dstu], tag=f"{tg}{kind}s{h}_{cc2}_{tc}")
        proj_conv("q", "L1_attn_b_Wq", "L1_attn_b_conv_q", 2, h * 2, 0)
        proj_conv("k", "L1_attn_b_Wk", "L1_attn_b_conv_k", 2, h * 2, 4096)
        proj_conv("v", "L1_attn_b_Wv", "L1_attn_b_conv_v", 4, h * 4, 8192)
        e256 = c["const_col"](2.56e-4)
        e1m6 = c["const_col"](1e-6)
        for kind, off, scl, bias in (("q", 0, 256.0, e256), ("k", 4096, 1.0, e1m6)):
            for tc in range(4):
                pss = PSF[2]
                for cc2 in range(2):
                    sl = BIGA[:, off + cc2 * 2048 + tc * 512:
                              off + cc2 * 2048 + (tc + 1) * 512]
                    sq = BF16S[:, 0:512]
                    P.op("S", lambda e, sq=sq, sl=sl: e.activation(sq, sl, AF.Square),
                         reads=[sl], writes=[sq], tag=f"{tg}{kind}sq{h}_{tc}_{cc2}")
                    P.op("T", lambda e, pss=pss, sq=sq, cc2=cc2:
                         e.matmul(pss[0:1, 0:512], ONEB[:, 0:1], sq,
                                  start=(cc2 == 0), stop=(cc2 == 1)),
                         reads=[ONEB[:, 0:1], sq] +
                               ([pss[0:1, 0:512]] if cc2 else []),
                         writes=[pss[0:1, 0:512]], tag=f"{tg}{kind}ss{h}_{tc}_{cc2}")
                rr = F32S[0:1, 2048:2560]
                P.op("S", lambda e, rr=rr, pss=pss, bias=bias, scl=scl:
                     e.activation(rr, pss[0:1, 0:512], AF.Sqrt, bias=bias[0:1, :],
                                  scale=scl),
                     reads=[pss[0:1, 0:512], bias], writes=[rr],
                     tag=f"{tg}{kind}sd{h}_{tc}")
                P.op("V", lambda e, rr=rr: e.reciprocal(rr, rr), reads=[rr],
                     writes=[rr], tag=f"{tg}{kind}sr{h}_{tc}")
                rrb = BF16S[0:1, 512:1024]
                P.op("V", lambda e, rrb=rrb, rr=rr: e.tensor_copy(rrb, rr),
                     reads=[rr], writes=[rrb], tag=f"{tg}{kind}sc{h}_{tc}")
                psb = PSF[3]
                P.op("T", lambda e, psb=psb, rrb=rrb:
                     e.matmul(psb[:, 0:512], ONEB[0:1, 0:128], rrb,
                              start=True, stop=True),
                     reads=[ONEB[0:1, 0:128], rrb], writes=[psb[:, 0:512]],
                     tag=f"{tg}{kind}sb{h}_{tc}")
                for cc2 in range(2):
                    sl = BIGA[:, off + cc2 * 2048 + tc * 512:
                              off + cc2 * 2048 + (tc + 1) * 512]
                    P.op("V", lambda e, sl=sl, psb=psb:
                         e.tensor_tensor(sl, sl, psb[:, 0:512], ALU.mult),
                         reads=[sl, psb[:, 0:512]], writes=[sl],
                         tag=f"{tg}{kind}nm{h}_{tc}_{cc2}")
        for cc2 in range(2):
            for tc16 in range(16):
                pb = PSB[tc16 % 2]
                blk = BIGA[:, 4096 + cc2 * 2048 + tc16 * 128:
                           4096 + cc2 * 2048 + (tc16 + 1) * 128]
                P.op("T", lambda e, pb=pb, blk=blk:
                     e.transpose(pb[:, 0:128], blk, IDB[:, :]),
                     reads=[blk, IDB[:, :]], writes=[pb[:, 0:128]],
                     tag=f"{tg}ktr{h}_{cc2}_{tc16}")
                dst = BIGB[:, 8192 + tc16 * 256 + cc2 * 128:
                           8192 + tc16 * 256 + (cc2 + 1) * 128]
                P.op("V", lambda e, dst=dst, pb=pb: e.tensor_copy(dst, pb[:, 0:128]),
                     reads=[pb[:, 0:128]], writes=[dst], tag=f"{tg}kte{h}_{cc2}_{tc16}")
        for cc2 in range(4):
            for tc16 in range(16):
                pb = PSB[tc16 % 2]
                blk = BIGA[:, 8192 + cc2 * 2048 + tc16 * 128:
                           8192 + cc2 * 2048 + (tc16 + 1) * 128]
                P.op("T", lambda e, pb=pb, blk=blk:
                     e.transpose(pb[:, 0:128], blk, IDB[:, :]),
                     reads=[blk, IDB[:, :]], writes=[pb[:, 0:128]],
                     tag=f"{tg}vtr{h}_{cc2}_{tc16}")
                dst = BIGB[:, tc16 * 512 + cc2 * 128:tc16 * 512 + (cc2 + 1) * 128]
                P.op("V", lambda e, dst=dst, pb=pb: e.tensor_copy(dst, pb[:, 0:128]),
                     reads=[pb[:, 0:128]], writes=[dst], tag=f"{tg}vte{h}_{cc2}_{tc16}")
        src = W["L1_attn_b_Wo"].ap()[h * 512:(h + 1) * 512, :]
        src = src.rearrange("(cc p) m -> p cc m", p=128)
        P.dma(KVC[:, 0:4096].rearrange("p (cc m) -> p cc m", m=1024), src,
              queue="G2", tag=f"{tg}wo{h}")
        src = W["L1_attn_b_Wg"].ap()[:, h * 512:(h + 1) * 512]
        src = src.rearrange("(dc p) r -> p dc r", p=128)
        P.dma(BIGB[:, 12288:16384].rearrange("p (dc r) -> p dc r", r=512), src,
              queue="G2", tag=f"{tg}wgh{h}")
        P.op("V", lambda e: e.memset(F32S[:, 0:1024], 0.0),
             writes=[F32S[:, 0:1024]], tag=f"{tg}s0{h}")
        P.op("V", lambda e: e.memset(BF16S[:, 2048:3072], 0.0),
             writes=[BF16S[:, 2048:3072]], tag=f"{tg}s0b{h}")
        for ci in range(16):
            row = h * 16 + ci
            stg = F32S[0:4, 2304:2432]
            src = bass.AP(F32S, row * 3072 + PK, [[3072, 1], [128, 4], [1, 128]])
            P.dma(bass.AP(F32S, 2304, [[3072, 4], [1, 128]]), src, queue="G2",
                  reads=[lam_hc, bet_hc, eL_hc, eC_hc], writes=[stg],
                  tag=f"{tg}st{h}_{ci}")
            psc = PSF[4]
            P.op("T", lambda e, stg=stg, psc=psc:
                 e.transpose(psc[:, 0:4], stg, IDF[0:4, 0:4]),
                 reads=[stg, IDF[:, :]], writes=[psc[:, 0:4]], tag=f"{tg}stt{h}_{ci}")
            cols = STAT[:, 64:68]
            P.op("V", lambda e, cols=cols, psc=psc: e.tensor_copy(cols, psc[:, 0:4]),
                 reads=[psc[:, 0:4]], writes=[cols], tag=f"{tg}ste{h}_{ci}")
            lamcol = STAT[:, 64:65]
            betcol = STAT[:, 65:66]
            eLcol = STAT[:, 66:67]
            eCcol = STAT[:, 67:68]
            nlam = STAT[:, 68:69]
            P.op("V", lambda e: e.tensor_scalar_mul(nlam, lamcol, -1.0),
                 reads=[lamcol], writes=[nlam], tag=f"{tg}nl{h}_{ci}")
            neL = STAT[:, 69:70]
            P.op("V", lambda e: e.tensor_scalar_mul(neL, eLcol, -1.0),
                 reads=[eLcol], writes=[neL], tag=f"{tg}nel{h}_{ci}")
            dcol = STAT[:, 70:71]
            P.op("V", lambda e: e.tensor_tensor(dcol, eLcol, eCcol, ALU.mult),
                 reads=[eLcol, eCcol], writes=[dcol], tag=f"{tg}dc{h}_{ci}")
            lrep = F32S[:, 1536 + 512:1536 + 640]
            P.dma(lrep, c["scr"]["s_lam"].ap()[row:row + 1, :]
                  .partition_broadcast(128),
                  queue="G2", writes=[lrep], tag=f"{tg}lr{h}_{ci}")
            ediff = F32S[:, 2176:2304]
            nbet = STAT[:, 72:73]
            P.op("V", lambda e, nbet=nbet, betcol=betcol:
                 e.tensor_scalar_mul(nbet, betcol, -1.0),
                 reads=[betcol], writes=[nbet], tag=f"{tg}nb{h}_{ci}")
            psk = PSF[0]
            psq = PSF[1]
            for cc2 in range(2):
                ksl = BIGA[:, 4096 + cc2 * 2048 + ci * 128:
                           4096 + cc2 * 2048 + (ci + 1) * 128]
                qsl = BIGA[:, cc2 * 2048 + ci * 128:cc2 * 2048 + (ci + 1) * 128]
                P.op("T", lambda e, psk=psk, ksl=ksl, cc2=cc2:
                     e.matmul(psk[:, 0:128], ksl, ksl, start=(cc2 == 0),
                              stop=(cc2 == 1)),
                     reads=[ksl] + ([psk[:, 0:128]] if cc2 else []),
                     writes=[psk[:, 0:128]], tag=f"{tg}kk{h}_{ci}_{cc2}")
                P.op("T", lambda e, psq=psq, ksl=ksl, qsl=qsl, cc2=cc2:
                     e.matmul(psq[:, 0:128], ksl, qsl, start=(cc2 == 0),
                              stop=(cc2 == 1)),
                     reads=[ksl, qsl] + ([psq[:, 0:128]] if cc2 else []),
                     writes=[psq[:, 0:128]], tag=f"{tg}qk{h}_{ci}_{cc2}")
            t1 = F32S[:, 1536:1664]
            P.op("V", lambda e, ediff=ediff, lrep=lrep:
                 e.scalar_tensor_tensor(ediff, lrep, nlam, mskN, ALU.add, ALU.add),
                 reads=[lrep, nlam, mskN], writes=[ediff], tag=f"{tg}en1{h}_{ci}")
            P.op("S", lambda e, ediff=ediff: e.activation(ediff, ediff, AF.Exp),
                 reads=[ediff], writes=[ediff], tag=f"{tg}en2{h}_{ci}")
            P.op("V", lambda e, t1=t1, ediff=ediff, psk=psk:
                 e.tensor_tensor(t1, ediff, psk[:, 0:128], ALU.mult),
                 reads=[ediff, psk[:, 0:128]], writes=[t1], tag=f"{tg}gm2{h}_{ci}")
            Bt = BF16S[:, 0:128]
            P.op("S", lambda e, Bt=Bt, t1=t1, nbet=nbet:
                 e.activation(Bt, t1, AF.Copy, scale=nbet),
                 reads=[t1, nbet], writes=[Bt], tag=f"{tg}gm3{h}_{ci}")
            Mt = BF16S[:, 640:768]
            P.op("V", lambda e, ediff=ediff, lrep=lrep:
                 e.scalar_tensor_tensor(ediff, lrep, nlam, mskM, ALU.add, ALU.add),
                 reads=[lrep, nlam, mskM], writes=[ediff], tag=f"{tg}em1{h}_{ci}")
            P.op("S", lambda e, ediff=ediff: e.activation(ediff, ediff, AF.Exp),
                 reads=[ediff], writes=[ediff], tag=f"{tg}em2{h}_{ci}")
            P.op("V", lambda e, Mt=Mt, ediff=ediff, psq=psq:
                 e.tensor_tensor(Mt, ediff, psq[:, 0:128], ALU.mult),
                 reads=[ediff, psq[:, 0:128]], writes=[Mt], tag=f"{tg}mt2{h}_{ci}")
            pb = PSB[0]
            P.op("T", lambda e, pb=pb, Bt=Bt:
                 e.transpose(pb[:, 0:128], Bt, IDB[:, :]),
                 reads=[Bt, IDB[:, :]], writes=[pb[:, 0:128]], tag=f"{tg}c0t{h}_{ci}")
            Ct = BF16S[:, 128:256]
            P.op("V", lambda e, Ct=Ct, pb=pb: e.tensor_copy(Ct, pb[:, 0:128]),
                 reads=[pb[:, 0:128]], writes=[Ct], tag=f"{tg}c0e{h}_{ci}")
            psu = PSF[2]
            for cc2 in range(2):
                ksl = BIGA[:, 4096 + cc2 * 2048 + ci * 128:
                           4096 + cc2 * 2048 + (ci + 1) * 128]
                sbf = BF16S[:, 2048 + cc2 * 512:2048 + (cc2 + 1) * 512]
                P.op("T", lambda e, psu=psu, ksl=ksl, sbf=sbf, cc2=cc2:
                     e.matmul(psu[:, 0:512], ksl, sbf, start=(cc2 == 0),
                              stop=(cc2 == 1)),
                     reads=[ksl, sbf] + ([psu[:, 0:512]] if cc2 else []),
                     writes=[psu[:, 0:512]], tag=f"{tg}u0m{h}_{ci}_{cc2}")
            Y = F32S[:, 1024:1536]
            vsl = BIGB[:, ci * 512:(ci + 1) * 512]
            P.op("V", lambda e, Y=Y, psu=psu, vsl=vsl:
                 e.scalar_tensor_tensor(Y, psu[:, 0:512], neL, vsl,
                                        ALU.mult, ALU.add),
                 reads=[psu[:, 0:512], neL, vsl], writes=[Y], tag=f"{tg}u0{h}_{ci}")
            bslots = [(0, 128), (256, 384)]
            cslots = [(128, 256), (384, 512)]
            for k in range(7):
                Ybf = BF16S[:, 1024:1536]
                P.op("V", lambda e, Ybf=Ybf, Y=Y: e.tensor_copy(Ybf, Y),
                     reads=[Y], writes=[Ybf], tag=f"{tg}yb{h}_{ci}_{k}")
                pst = PSF[3]
                Bcur = BF16S[:, bslots[k % 2][0]:bslots[k % 2][1]]
                P.op("T", lambda e, pst=pst, Bcur=Bcur, Ybf=Ybf:
                     e.matmul(pst[:, 0:512], Bcur, Ybf, start=True, stop=True),
                     reads=[Bcur, Ybf], writes=[pst[:, 0:512]],
                     tag=f"{tg}ap{h}_{ci}_{k}")
                P.op("V", lambda e, Y=Y, pst=pst:
                     e.tensor_tensor(Y, Y, pst[:, 0:512], ALU.add),
                     reads=[Y, pst[:, 0:512]], writes=[Y], tag=f"{tg}ya{h}_{ci}_{k}")
                if k < 6:
                    Ccur = BF16S[:, cslots[k % 2][0]:cslots[k % 2][1]]
                    Bn = BF16S[:, bslots[(k + 1) % 2][0]:bslots[(k + 1) % 2][1]]
                    Cn = BF16S[:, cslots[(k + 1) % 2][0]:cslots[(k + 1) % 2][1]]
                    ps1 = PSF[4]
                    ps2 = PSF[5]
                    P.op("T", lambda e, ps1=ps1, Ccur=Ccur, Bcur=Bcur:
                         e.matmul(ps1[:, 0:128], Ccur, Bcur, start=True, stop=True),
                         reads=[Ccur, Bcur], writes=[ps1[:, 0:128]],
                         tag=f"{tg}sqb{h}_{ci}_{k}")
                    P.op("T", lambda e, ps2=ps2, Bcur=Bcur, Ccur=Ccur:
                         e.matmul(ps2[:, 0:128], Bcur, Ccur, start=True, stop=True),
                         reads=[Bcur, Ccur], writes=[ps2[:, 0:128]],
                         tag=f"{tg}sqc{h}_{ci}_{k}")
                    P.op("V", lambda e, Bn=Bn, ps1=ps1: e.tensor_copy(Bn, ps1[:, 0:128]),
                         reads=[ps1[:, 0:128]], writes=[Bn], tag=f"{tg}sbe{h}_{ci}_{k}")
                    P.op("V", lambda e, Cn=Cn, ps2=ps2: e.tensor_copy(Cn, ps2[:, 0:128]),
                         reads=[ps2[:, 0:128]], writes=[Cn], tag=f"{tg}sce{h}_{ci}_{k}")
            Wbf = BF16S[:, 512:1024] if False else BF16S[:, 768:1280]
            Wbf = BF16S[:, 3072:3584]
            P.op("S", lambda e, Wbf=Wbf, Y=Y:
                 e.activation(Wbf, Y, AF.Copy, scale=betcol),
                 reads=[Y, betcol], writes=[Wbf], tag=f"{tg}w{h}_{ci}")
            Wt = BF16S[:, 3584:4096]
            P.op("V", lambda e, Wt=Wt, Wbf=Wbf, eCcol=eCcol:
                 e.tensor_scalar_mul(Wt, Wbf, eCcol),
                 reads=[Wbf, eCcol], writes=[Wt], tag=f"{tg}wt{h}_{ci}")
            pso = PSF[2]
            for cc2 in range(2):
                qsl = BIGA[:, cc2 * 2048 + ci * 128:cc2 * 2048 + (ci + 1) * 128]
                sbf = BF16S[:, 2048 + cc2 * 512:2048 + (cc2 + 1) * 512]
                P.op("T", lambda e, pso=pso, qsl=qsl, sbf=sbf, cc2=cc2:
                     e.matmul(pso[:, 0:512], qsl, sbf, start=(cc2 == 0),
                              stop=(cc2 == 1)),
                     reads=[qsl, sbf] + ([pso[:, 0:512]] if cc2 else []),
                     writes=[pso[:, 0:512]], tag=f"{tg}om{h}_{ci}_{cc2}")
            o_t = F32S[:, 1536:2048]
            P.op("S", lambda e, o_t=o_t, pso=pso:
                 e.activation(o_t, pso[:, 0:512], AF.Copy, scale=eLcol),
                 reads=[pso[:, 0:512], eLcol], writes=[o_t], tag=f"{tg}os{h}_{ci}")
            psm = PSF[3]
            P.op("T", lambda e, psm=psm, Mt=Mt, Wbf=Wbf:
                 e.matmul(psm[:, 0:512], Mt, Wbf, start=True, stop=True),
                 reads=[Mt, Wbf], writes=[psm[:, 0:512]], tag=f"{tg}mw{h}_{ci}")
            P.op("V", lambda e, o_t=o_t, psm=psm:
                 e.tensor_tensor(o_t, o_t, psm[:, 0:512], ALU.add),
                 reads=[o_t, psm[:, 0:512]], writes=[o_t], tag=f"{tg}oa{h}_{ci}")
            for cc2 in range(2):
                pss2 = PSF[4]
                ktm = BIGB[:, 8192 + ci * 256 + cc2 * 128:
                           8192 + ci * 256 + (cc2 + 1) * 128]
                P.op("T", lambda e, pss2=pss2, ktm=ktm, Wt=Wt:
                     e.matmul(pss2[:, 0:512], ktm, Wt, start=True, stop=True),
                     reads=[ktm, Wt], writes=[pss2[:, 0:512]],
                     tag=f"{tg}su{h}_{ci}_{cc2}")
                ssl = F32S[:, cc2 * 512:(cc2 + 1) * 512]
                P.op("V", lambda e, ssl=ssl, dcol=dcol, pss2=pss2:
                     e.scalar_tensor_tensor(ssl, ssl, dcol, pss2[:, 0:512],
                                            ALU.mult, ALU.add),
                     reads=[ssl, dcol, pss2[:, 0:512]], writes=[ssl],
                     tag=f"{tg}sd{h}_{ci}_{cc2}")
                sbf = BF16S[:, 2048 + cc2 * 512:2048 + (cc2 + 1) * 512]
                P.op("V", lambda e, sbf=sbf, ssl=ssl: e.tensor_copy(sbf, ssl),
                     reads=[ssl], writes=[sbf], tag=f"{tg}sc{h}_{ci}_{cc2}")
            ss2 = STAT[:, 71:72]
            sqo = F32S[:, 2176:2304] if False else None
            sq_t = F32S[:, 2048:2560]
            P.op("S", lambda e, sq_t=sq_t, o_t=o_t, ss2=ss2:
                 e.activation(sq_t, o_t, AF.Square, accum_out=ss2),
                 reads=[o_t], writes=[sq_t, ss2], tag=f"{tg}osq{h}_{ci}")
            eps5 = c["const_col"](512e-5)
            P.op("S", lambda e, ss2=ss2, eps5=eps5:
                 e.activation(ss2, ss2, AF.Sqrt, scale=1.0, bias=eps5),
                 reads=[ss2, eps5], writes=[ss2], tag=f"{tg}ostd{h}_{ci}")
            P.op("V", lambda e, ss2=ss2: e.reciprocal(ss2, ss2),
                 reads=[ss2], writes=[ss2], tag=f"{tg}orc{h}_{ci}")
            P.op("V", lambda e, ss2=ss2:
                 e.tensor_scalar_mul(ss2, ss2, float(np.sqrt(512.0))),
                 reads=[ss2], writes=[ss2], tag=f"{tg}or2{h}_{ci}")
            nwrow = BIAS[:, 2304:2816]
            if ci == 0:
                P.dma(nwrow, W["L1_attn_b_norm_w"].ap().unsqueeze(0)
                      .partition_broadcast(128), queue="G2", tag=f"{tg}nw{h}")
            psg = PSF[5]
            for dc in range(8):
                nsl = NFM[:, dc * T + ci * 128:dc * T + (ci + 1) * 128]
                wsl = BIGB[:, 12288 + dc * 512:12288 + (dc + 1) * 512]
                P.op("T", lambda e, psg=psg, nsl=nsl, wsl=wsl, dc=dc:
                     e.matmul(psg[:, 0:512], nsl, wsl, start=(dc == 0),
                              stop=(dc == 7)),
                     reads=[nsl, wsl] + ([psg[:, 0:512]] if dc else []),
                     writes=[psg[:, 0:512]], tag=f"{tg}gm{h}_{ci}_{dc}")
            gsil = BF16S[:, 1536:2048]
            P.op("S", lambda e, gsil=gsil, psg=psg:
                 e.activation(gsil, psg[:, 0:512], AF.Silu),
                 reads=[psg[:, 0:512]], writes=[gsil], tag=f"{tg}gs{h}_{ci}")
            og = BF16S[:, 1536:2048]
            P.op("V", lambda e, og=og, o_t=o_t, ss2=ss2:
                 e.tensor_scalar_mul(o_t, o_t, ss2),
                 reads=[o_t, ss2], writes=[o_t], tag=f"{tg}on{h}_{ci}")
            P.op("V", lambda e, o_t=o_t, nwrow=nwrow:
                 e.tensor_tensor(o_t, o_t, nwrow, ALU.mult),
                 reads=[o_t, nwrow], writes=[o_t], tag=f"{tg}onw{h}_{ci}")
            P.op("V", lambda e, og=og, o_t=o_t:
                 e.tensor_tensor(og, og, o_t, ALU.mult),
                 reads=[og, o_t], writes=[og], tag=f"{tg}og{h}_{ci}")
            for j in range(4):
                pb = PSB[j % 2]
                blk = og[:, j * 128:(j + 1) * 128]
                P.op("T", lambda e, pb=pb, blk=blk:
                     e.transpose(pb[:, 0:128], blk, IDB[:, :]),
                     reads=[blk, IDB[:, :]], writes=[pb[:, 0:128]],
                     tag=f"{tg}ot{h}_{ci}_{j}")
                dst = BF16S[:, 1024 + j * 128:1024 + (j + 1) * 128]
                P.op("V", lambda e, dst=dst, pb=pb: e.tensor_copy(dst, pb[:, 0:128]),
                     reads=[pb[:, 0:128]], writes=[dst], tag=f"{tg}ote{h}_{ci}_{j}")
            ogf = BF16S[:, 1024:1536]
            for dmh in range(2):
                pso2 = PSF[0 + dmh]
                for j in range(4):
                    lh = ogf[:, j * 128:(j + 1) * 128]
                    rh = KVC[:, j * 1024 + dmh * 512:j * 1024 + dmh * 512 + 512]
                    P.op("T", lambda e, pso2=pso2, lh=lh, rh=rh, j=j:
                         e.matmul(pso2[:, 0:512], lh, rh, start=(j == 0),
                                  stop=(j == 3)),
                         reads=[lh, rh] + ([pso2[:, 0:512]] if j else []),
                         writes=[pso2[:, 0:512]], tag=f"{tg}fo{h}_{ci}_{dmh}_{j}")
                rsl = RESID[:, ci * D + dmh * 512:ci * D + dmh * 512 + 512]
                P.op("V", lambda e, rsl=rsl, pso2=pso2:
                     e.tensor_tensor(rsl, rsl, pso2[:, 0:512], ALU.add),
                     reads=[rsl, pso2[:, 0:512]], writes=[rsl],
                     tag=f"{tg}fr{h}_{ci}_{dmh}")


def fp4_quant_tile(c, src_f32, cw, sgn_sl, acca_sl, accb_sl, out_sl, tag):
    P = c["P"]
    F32S, KVC, BF16S = c["F32S"], c["KVC"], c["BF16S"]
    nb = cw // 16
    sgn = KVC[:, sgn_sl:sgn_sl + cw]
    acca = KVC[:, acca_sl:acca_sl + cw]
    accb = KVC[:, accb_sl:accb_sl + cw]
    out = BF16S[:, out_sl:out_sl + cw]
    av = F32S[:, 1024:1024 + cw]
    bm = F32S[:, 2048:2048 + nb]
    scale = F32S[:, 2048 + 176:2048 + 176 + nb]
    rcp = F32S[:, 2048 + 352:2048 + 352 + nb]
    src3 = src_f32.rearrange("p (a b) -> p a b", b=16)
    P.op("S", lambda e: e.sign(sgn, src_f32), reads=[src_f32], writes=[sgn],
         tag=f"{tag}sgn")
    P.op("V", lambda e: e.reduce_max(bm, src3, axis=AXX, apply_absolute_value=True),
         reads=[src_f32], writes=[bm], tag=f"{tag}bm")
    P.op("V", lambda e: e.tensor_scalar(scale, bm, 1.0 / 6.0, 1e-12,
                                        ALU.mult, ALU.max),
         reads=[bm], writes=[scale], tag=f"{tag}sc")
    P.op("V", lambda e: e.reciprocal(rcp, scale), reads=[scale], writes=[rcp],
         tag=f"{tag}rcp")
    P.op("S", lambda e: e.activation(av, src_f32, AF.Abs), reads=[src_f32],
         writes=[av], tag=f"{tag}abs")
    av3 = av.rearrange("p (a b) -> p a b", b=16)
    rcp3 = rcp.unsqueeze(2).broadcast_to([128, nb, 16])
    P.op("V", lambda e: e.tensor_tensor(av3, av3, rcp3, ALU.mult),
         reads=[av, rcp], writes=[av], tag=f"{tag}vn")
    P.op("V", lambda e: e.tensor_single_scalar(acca, av, 0.25, ALU.is_gt),
         reads=[av], writes=[acca], tag=f"{tag}l0")
    for th in (0.75, 1.25, 1.75):
        P.op("V", lambda e, th=th: e.scalar_tensor_tensor(
            acca, av, th, acca, ALU.is_gt, ALU.add),
            reads=[av, acca], writes=[acca], tag=f"{tag}l{th}")
    P.op("V", lambda e: e.tensor_single_scalar(accb, av, 5.0, ALU.is_gt),
         reads=[av], writes=[accb], tag=f"{tag}h0")
    for th in (5.0, 3.5, 2.5):
        P.op("V", lambda e, th=th: e.scalar_tensor_tensor(
            accb, av, th, accb, ALU.is_gt, ALU.add),
            reads=[av, accb], writes=[accb], tag=f"{tag}h{th}")
    P.op("V", lambda e: e.scalar_tensor_tensor(acca, acca, 0.5, accb,
                                               ALU.mult, ALU.add),
         reads=[acca, accb], writes=[acca], tag=f"{tag}q")
    P.op("V", lambda e: e.tensor_tensor(out, acca, sgn, ALU.mult),
         reads=[acca, sgn], writes=[out], tag=f"{tag}qs")
    sc3 = scale.unsqueeze(2).broadcast_to([128, nb, 16])
    out3 = out.rearrange("p (a b) -> p a b", b=16)
    P.op("V", lambda e: e.tensor_tensor(out3, out3, sc3, ALU.mult),
         reads=[out, scale], writes=[out], tag=f"{tag}qq")
    return out


def tern_quant_tile(c, src_f32, cw, thr_col, out_sl, tag):
    P = c["P"]
    F32S, KVC, BF16S = c["F32S"], c["KVC"], c["BF16S"]
    sgn = KVC[:, 0:cw]
    m = KVC[:, 1024:1024 + cw]
    out = BF16S[:, out_sl:out_sl + cw]
    av = F32S[:, 1024:1024 + cw]
    P.op("S", lambda e: e.sign(sgn, src_f32), reads=[src_f32], writes=[sgn],
         tag=f"{tag}sgn")
    P.op("S", lambda e: e.activation(av, src_f32, AF.Abs), reads=[src_f32],
         writes=[av], tag=f"{tag}abs")
    P.op("V", lambda e: e.tensor_single_scalar(m, av, thr_col, ALU.is_gt),
         reads=[av, thr_col], writes=[m], tag=f"{tag}gt")
    P.op("V", lambda e: e.tensor_tensor(out, m, sgn, ALU.mult),
         reads=[m, sgn], writes=[out], tag=f"{tag}t")
    return out


def prep_stream_transpose(c, wname, R, C, dst, quant=None, thr_col=None, tag=""):
    P = c["P"]
    F32S, BF16S, IDB, STRM = c["F32S"], c["BF16S"], c["IDB"], c["STRM"]
    w = c["W"][wname] if wname in c["W"] else c["scr"][wname]
    nrt = (R + 127) // 128
    pb_i = 0
    for rt in range(nrt):
        r0 = rt * 128
        pr = min(128, R - r0)
        for cc0 in range(0, C, 1024):
            cw = min(1024, C - cc0)
            wt = F32S[:pr, 0:cw]
            P.dma(wt, w.ap()[r0:r0 + pr, cc0:cc0 + cw], queue="Y",
                  tag=f"{tag}ld{rt}_{cc0}")
            if quant == "fp4":
                qb = fp4_quant_tile(c, wt, cw, 0, 1024, 2048, 0,
                                    f"{tag}q{rt}_{cc0}_")[:pr, :]
            elif quant == "tern":
                qb = tern_quant_tile(c, wt, cw, thr_col, 0,
                                     f"{tag}t{rt}_{cc0}_")[:pr, :]
            else:
                qb = BF16S[:pr, 0:cw]
                P.op("S", lambda e, wt=wt, qb=qb: e.copy(qb, wt),
                     reads=[wt], writes=[qb], tag=f"{tag}cast{rt}_{cc0}")
            nblk = (cw + 127) // 128
            for g0 in range(0, nblk, 8):
                gn = min(8, nblk - g0)
                pb = c["PSB"][pb_i % 2]
                pb_i += 1
                for j in range(gn):
                    b = g0 + j
                    bw = min(128, cw - b * 128)
                    blk = qb[:pr, b * 128:b * 128 + bw]
                    dstp = pb[0:bw, j * 128:j * 128 + pr]
                    idb = IDB[0:pr, 0:pr]
                    P.op("T", lambda e, blk=blk, dstp=dstp, idb=idb:
                         e.transpose(dstp, blk, idb),
                         reads=[blk, IDB[:, :]], writes=[dstp],
                         tag=f"{tag}tr{rt}_{cc0}_{b}")
                ev = STRM[:, 0:1024]
                P.op("V", lambda e, pb=pb, ev=ev: e.tensor_copy(ev, pb[:, 0:1024]),
                     reads=[pb[:, 0:1024]], writes=[ev], tag=f"{tag}ev{rt}_{cc0}_{g0}")
                if cw % 128 == 0:
                    dr = dst.ap()[cc0 + g0 * 128: cc0 + g0 * 128 + gn * 128,
                                  r0:r0 + pr]
                    dr = dr.rearrange("(g p) r -> p g r", p=128)
                    srcap = ev.rearrange("p (g r) -> p g r", r=128)[:, 0:gn, 0:pr]
                    P.dma(dr, srcap, queue="G2", reads=[ev], writes=[],
                          tag=f"{tag}st{rt}_{cc0}_{g0}")
                else:
                    for j in range(gn):
                        b = g0 + j
                        bw = min(128, cw - b * 128)
                        dr = dst.ap()[cc0 + b * 128: cc0 + b * 128 + bw,
                                      r0:r0 + pr]
                        srcap = ev[0:bw, j * 128:j * 128 + pr]
                        P.dma(dr, srcap, queue="G2", reads=[ev], writes=[],
                              tag=f"{tag}st{rt}_{cc0}_{b}")


def preprocess_weights(c):
    P, W, scr = c["P"], c["W"], c["scr"]
    F32S, STAT, ONEF = c["F32S"], c["STAT"], c["ONEF"]
    PSF = c["PSF"]
    def tern_scale(wname, R, C, thr_slot, ws_slot, tag):
        acc = STAT[:, thr_slot + 2:thr_slot + 3]
        P.op("V", lambda e: e.memset(acc, 0.0), writes=[acc], tag=f"{tag}z")
        for rt in range(R // 128):
            for cc0 in range(0, C, 1024):
                cw = min(1024, C - cc0)
                wt = F32S[:, 0:cw]
                P.dma(wt, W[wname].ap()[rt * 128:(rt + 1) * 128, cc0:cc0 + cw],
                      queue="Y", tag=f"{tag}ld{rt}_{cc0}")
                part = STAT[:, thr_slot + 3:thr_slot + 4]
                P.op("V", lambda e, wt=wt, part=part:
                     e.reduce_sum(part, wt, axis=AXX, apply_absolute_value=True),
                     reads=[wt], writes=[part], tag=f"{tag}r{rt}_{cc0}")
                P.op("V", lambda e, part=part: e.tensor_tensor(acc, acc, part, ALU.add),
                     reads=[part, acc], writes=[acc], tag=f"{tag}a{rt}_{cc0}")
        ps = PSF[5]
        P.op("V", lambda e: e.memset(ONEF[:, 0:1], 1.0), writes=[ONEF[:, 0:1]],
             tag=f"{tag}of")
        P.op("T", lambda e: e.matmul(ps[0:1, 0:1], acc, ONEF[:, 0:1],
                                     start=True, stop=True),
             reads=[acc, ONEF[:, 0:1]], writes=[ps[0:1, 0:1]], tag=f"{tag}mm")
        tot = STAT[0:1, thr_slot:thr_slot + 1]
        P.op("V", lambda e: e.tensor_scalar(tot, ps[0:1, 0:1], 1.0 / (R * C), 1e-5,
                                            ALU.mult, ALU.max),
             reads=[ps[0:1, 0:1]], writes=[tot], tag=f"{tag}ws")
        row1 = F32S[0:1, 0:128]
        P.op("V", lambda e: e.memset(row1, 1.0), writes=[row1], tag=f"{tag}r1")
        P.op("T", lambda e: e.matmul(ps[:, 4:5], row1, tot, start=True, stop=True),
             reads=[row1, tot], writes=[ps[:, 4:5]], tag=f"{tag}bc")
        wscol = STAT[:, ws_slot:ws_slot + 1]
        thrcol = STAT[:, thr_slot + 1:thr_slot + 2]
        P.op("V", lambda e: e.tensor_copy(wscol, ps[:, 4:5]),
             reads=[ps[:, 4:5]], writes=[wscol], tag=f"{tag}wsc")
        P.op("V", lambda e: e.tensor_scalar_mul(thrcol, wscol, 0.5),
             reads=[wscol], writes=[thrcol], tag=f"{tag}thr")
        return thrcol, wscol

    pp = os.environ.get("ATHENA_PP", "all")
    def on(k):
        return pp == "all" or k in pp.split(",")

    if on("ts"):
        thr1, ws1 = tern_scale("L1_ffn_W1", 4096, 1024, 100, 104, "ts1")
        thr2, ws2 = tern_scale("L1_ffn_W2", 1024, 4096, 105, 109, "ts2")
        c["ws1_col"] = ws1
        c["ws2_col"] = ws2

    if on("ip"):
        prep_stream_transpose(c, "L0_attn_a_in_proj_w", 4096, 1024,
                              scr["s_inprojT"], tag="pIP")
    if on("op"):
        prep_stream_transpose(c, "L0_attn_a_out_proj_w", 1024, 2048,
                              scr["s_outprojT"], tag="pOP")
    if on("xw"):
        prep_stream_transpose(c, "L0_attn_a_x_proj_w", 192, 512, scr["s_xwT"],
                              tag="pXW")
    if on("dt"):
        prep_stream_transpose(c, "L0_attn_a_dt_proj_w", 2048, 16, scr["s_dtwT"],
                              tag="pDT")
    if on("wg"):
        prep_stream_transpose(c, "L0_ffn_w_gate", 2816, 1024, scr["s_wgT"],
                              quant="fp4", tag="pWG")
        prep_stream_transpose(c, "L0_ffn_w_up", 2816, 1024, scr["s_wuT"],
                              quant="fp4", tag="pWU")
        prep_stream_transpose(c, "L0_ffn_w_down", 1024, 2816, scr["s_wdT"],
                              quant="fp4", tag="pWD")
    if on("ts"):
        prep_stream_transpose(c, "L1_ffn_W1", 4096, 1024, scr["s_w1tT"],
                              quant="tern", thr_col=thr1, tag="pW1")
        prep_stream_transpose(c, "L1_ffn_W2", 1024, 4096, scr["s_w2tT"],
                              quant="tern", thr_col=thr2, tag="pW2")


INPUT_SHAPES = [
    ("L0_attn_a_in_proj_w", (4096, 1024)),
    ("L0_attn_a_conv_w", (2048, 1, 4)),
    ("L0_attn_a_conv_b", (2048,)),
    ("L0_attn_a_x_proj_w", (192, 512)),
    ("L0_attn_a_dt_proj_w", (2048, 16)),
    ("L0_attn_a_dt_proj_b", (2048,)),
    ("L0_attn_a_A_log", (2048, 16)),
    ("L0_attn_a_Dskip", (2048,)),
    ("L0_attn_a_out_proj_w", (1024, 2048)),
    ("L0_attn_b_W_toP", (1024, 512)),
    ("L0_attn_b_W_F", (1024, 512)),
    ("L0_attn_b_W_lam", (512, 64)),
    ("L0_attn_b_B_mat", (512, 64)),
    ("L0_attn_b_C_mat", (64, 512)),
    ("L0_attn_b_W_fromP", (512, 1024)),
    ("L0_ffn_w_gate", (2816, 1024)),
    ("L0_ffn_w_up", (2816, 1024)),
    ("L0_ffn_w_down", (1024, 2816)),
    ("L1_attn_a_W_kv", (1024, 256)),
    ("L1_attn_a_g_kv", (256,)),
    ("L1_attn_a_W_kdec", (256, 1024)),
    ("L1_attn_a_W_vdec", (256, 1024)),
    ("L1_attn_a_W_q", (1024, 512)),
    ("L1_attn_a_W_qdec", (512, 1024)),
    ("L1_attn_a_W_krope", (1024, 512)),
    ("L1_attn_a_W_qrope", (512, 512)),
    ("L1_attn_a_W_o", (1024, 1024)),
    ("L1_attn_b_ln_w", (1024,)),
    ("L1_attn_b_ln_b", (1024,)),
    ("L1_attn_b_Wq", (1024, 1024)),
    ("L1_attn_b_Wk", (1024, 1024)),
    ("L1_attn_b_Wv", (1024, 2048)),
    ("L1_attn_b_conv_q", (1024, 4)),
    ("L1_attn_b_conv_k", (1024, 4)),
    ("L1_attn_b_conv_v", (2048, 4)),
    ("L1_attn_b_Wb", (1024, 4)),
    ("L1_attn_b_Wa", (1024, 4)),
    ("L1_attn_b_A_log", (4,)),
    ("L1_attn_b_dt_bias", (4,)),
    ("L1_attn_b_Wg", (1024, 2048)),
    ("L1_attn_b_norm_w", (512,)),
    ("L1_attn_b_Wo", (2048, 1024)),
    ("L1_ffn_W1", (4096, 1024)),
    ("L1_ffn_b1", (4096,)),
    ("L1_ffn_W2", (1024, 4096)),
    ("L1_ffn_b2", (4096,)),
    ("L0_attn_b_W_U_flat", (1024, 2048)),
    ("L0_attn_b_G", (2048, 512)),
]
INPUT_SHAPES = [(n, ((1024,) if n == "L1_ffn_b2" else s)) for (n, s) in INPUT_SHAPES]

AUX_BF16_SHAPES = [
    ("aux_ident_b", (128, 128)),
    ("aux_ones_b", (128, 128)),
    ("aux_cmask", (128, 2048)),
    ("aux_cos_q", (512, 2048)),
    ("aux_sin_q", (512, 2048)),
    ("aux_cos_k", (512, 2048)),
    ("aux_sin_k", (512, 2048)),
    ("aux_maskNstrict", (128, 128)),
    ("aux_sel16", (16, 2048)),
    ("aux_maskMincl", (128, 128)),
]


def _aux_arrays():
    aux = {}
    aux["aux_ident_b"] = np.eye(128, dtype=np.float32).astype(BF)
    aux["aux_ones_b"] = np.ones((128, 128), np.float32).astype(BF)
    m = np.zeros((128, 4 * 512), np.float32)
    for oi, off in enumerate((0, 128, 256, 384)):
        i = np.arange(128)[:, None]
        j = np.arange(512)[None, :]
        m[:, oi * 512:(oi + 1) * 512] = np.where(j <= i + off, 0.0, -1e9)
    aux["aux_cmask"] = m.astype(BF)
    dim = 32
    inv = 1.0 / (10000.0 ** (np.arange(0, dim, 2, dtype=np.float64) / dim))
    tt = np.arange(T, dtype=np.float64)
    fr = np.outer(tt, inv)
    cos = np.cos(np.concatenate([fr, fr], 1))
    sin = np.sin(np.concatenate([fr, fr], 1))
    sin_signed = sin.copy()
    sin_signed[:, :16] *= -1.0
    cosf = np.tile(cos.T, (16, 1)).astype(np.float32)
    sinf = np.tile(sin_signed.T, (16, 1)).astype(np.float32)
    s = 1.0 / np.sqrt(96.0)
    aux["aux_cos_q"] = (cosf * s).astype(BF)
    aux["aux_sin_q"] = (sinf * s).astype(BF)
    aux["aux_cos_k"] = cosf.astype(BF)
    aux["aux_sin_k"] = sinf.astype(BF)
    i = np.arange(128)[:, None]
    j = np.arange(128)[None, :]
    aux["aux_maskNstrict"] = np.where(j > i, 0.0, -60000.0).astype(np.float32).astype(BF)
    sel = np.zeros((16, 2048), np.float32)
    for n in range(16):
        sel[n, n*128:(n+1)*128] = 1.0
    aux["aux_sel16"] = sel.astype(BF)
    aux["aux_maskMincl"] = np.where(j >= i, 0.0, -60000.0).astype(np.float32).astype(BF)
    return aux


_NSUB = int(os.environ.get("ATHENA_NSUB", "8"))


def kernel(**inputs):
    xfull = np.ascontiguousarray(inputs["L0_attn_a_hidden_states"], np.float32)
    B = xfull.shape[0]
    nsub = _NSUB
    nc = build_program(nsub)

    aux = _aux_arrays()
    base = {}
    for name, shape in INPUT_SHAPES:
        if name == "L0_attn_b_W_U_flat":
            base[name] = np.ascontiguousarray(
                inputs["L0_attn_b_W_U"].reshape(1024, 2048), np.float32)
        elif name == "L0_attn_b_G":
            wv = np.asarray(inputs["L0_attn_b_W_V"], np.float32).reshape(-1)
            g = np.zeros((2048, 512), np.float32)
            g[np.arange(2048), np.arange(2048) // 4] = wv
            base[name] = g
        else:
            base[name] = np.ascontiguousarray(inputs[name], np.float32)
    base["aux_ident_f"] = np.eye(128, dtype=np.float32)
    for k, v in aux.items():
        base[k] = v

    in_maps = []
    for b in range(B):
        m = dict(base)
        m["x"] = np.ascontiguousarray(xfull[b], np.float32)
        in_maps.append(m)

    res = run_bass_kernel_spmd(nc, in_maps, core_ids=list(range(8)))
    out = np.stack([r["y"] for r in res.results], axis=0)
    return out.astype(np.float32)
